# version 12 (final)
# baseline (speedup 1.0000x reference)
import numpy as np
import concourse.bass as bass
import concourse.mybir as mybir
from concourse.bass_utils import run_bass_kernel_spmd

F32 = mybir.dt.float32
BF16 = mybir.dt.bfloat16
I8 = mybir.dt.int8
AF = mybir.ActivationFunctionType
OP = mybir.AluOpType

NEG_BIG = -1e30
NCH = 16

ENGINES = ("sync", "tensor", "vector", "scalar", "gpsimd")


class Sem:
    def __init__(self, handle):
        self.h = handle
        self.n = 0


class Eng:
    def __init__(self, name):
        self.name = name
        self.ops = []
        self.kinds = []

    def _push(self, fn_name, a, kw, incs):
        self.kinds.append(fn_name)
        def op(eng):
            inst = getattr(eng, fn_name)(*a, **kw)
            for (sem, k) in incs:
                inst.then_inc(sem.h, k)
        self.ops.append(op)

    def __getattr__(self, fn_name):
        if fn_name.startswith("_"):
            raise AttributeError(fn_name)

        def call(*a, inc=None, **kw):
            incs = []
            if inc is not None:
                for (sem, k) in inc:
                    sem.n += k
                    incs.append((sem, k))
            self._push(fn_name, a, kw, incs)
        return call

    def wait(self, sem, val=None):
        v = sem.n if val is None else val
        def op(eng):
            eng.wait_ge(sem.h, v)
        self.ops.append(op)
        self.kinds.append("wait")

    def sem_inc(self, sem, k=1):
        sem.n += k
        def op(eng):
            eng.nop().then_inc(sem.h, k)
        self.ops.append(op)
        self.kinds.append("nop")


def dinc(sem):
    return [(sem, 16)]


def inc1(sem, k=1):
    return [(sem, k)]


class Prog:
    def __init__(self, nc):
        self.nc = nc
        self.q = Eng("sync")
        self.t = Eng("tensor")
        self.v = Eng("vector")
        self.s = Eng("scalar")
        self.g = Eng("gpsimd")
        self._sem_guards = []

    def sem(self, name):
        guard = self.nc.semaphore(name)
        h = guard.__enter__()
        self._sem_guards.append(guard)
        return Sem(h)

    def finalize(self):
        nc = self.nc
        with nc.Block() as block:
            @block.sync
            def _(e):
                for op in self.q.ops:
                    op(e)

            @block.tensor
            def _(e):
                for op in self.t.ops:
                    op(e)

            @block.vector
            def _(e):
                for op in self.v.ops:
                    op(e)

            @block.scalar
            def _(e):
                for op in self.s.ops:
                    op(e)

            @block.gpsimd
            def _(e):
                for op in self.g.ops:
                    op(e)
        for gd in self._sem_guards:
            gd.__exit__(None, None, None)


def ts(i, n):
    return slice(i * n, (i + 1) * n)

def chunked(dram2d):
    return dram2d.rearrange("(c p) m -> p c m", p=128)


def sb3(sbuf_ap, m):
    return sbuf_ap.rearrange("p (c m) -> p c m", m=m)



PHASE_MARKS = []


def build_program(stage=99, dbg=()):
    nc = bass.Bass("TRN2", target_bir_lowering=False, debug=False)
    p = Prog(nc)
    PHASE_MARKS.clear()

    def mark(name):
        PHASE_MARKS.append((name, len(p.t.ops)))

    def din(name, shape, dtype=F32):
        return nc.dram_tensor(name, shape, dtype, kind="ExternalInput").ap()

    def dout(name, shape, dtype=F32):
        return nc.dram_tensor(name, shape, dtype, kind="ExternalOutput").ap()

    def dint(name, shape, dtype=F32):
        return nc.dram_tensor(name, shape, dtype, kind="Internal").ap()

    x_in = din("x_in", [2048, 1024])
    wqp = din("wqp", [4096, 1024])
    wkp = din("wkp", [1024, 4096])
    wvp = din("wvp", [1024, 4096])
    wo0 = din("wo0", [2048, 2048])
    bo_t = din("bo_t", [128, 16])
    wg8p = din("wg8p", [4096, 4096], I8)
    wu8p = din("wu8p", [4096, 4096], I8)
    wdp = din("wdp", [4096, 4096], I8)
    sg_t = din("sg_t", [128, 64]); su_t = din("su_t", [128, 64])
    sd_t = din("sd_t", [128, 16])
    wqd = din("wqd", [2048, 1536]); wkvd = din("wkvd", [2048, 512])
    wqwp = din("wqwp", [2048, 1152])
    wkwp = din("wkwp", [1024, 768])
    wvwp = din("wvwp", [1024, 768])
    wo1 = din("wo1", [3072, 2048])
    w1p = din("w1p", [5632, 2048])
    w2p = din("w2p", [5632, 2048])
    w3p = din("w3p", [5632, 2048])
    tok_off = din("tok_off", [1, 1], mybir.dt.int32)
    ident_bf = din("ident_bf", [128, 128], BF16)
    ones_f = din("ones_f", [1, 128])
    cosd = din("cosd", [64, 2048]); sind = din("sind", [64, 2048])
    mask_e = din("mask_e", [128, 256], BF16)
    mask_o = din("mask_o", [128, 256], BF16)

    y = dout("y", [2048, 1024])

    kvred_in = dint("kvred_in", [64, 2080])
    kvred_out = dint("kvred_out", [64, 2080])
    aout0_hi = dint("aout0_hi", [2048, 1024], BF16)
    aout0_lo = dint("aout0_lo", [2048, 1024], BF16)
    fuse_hi = dint("fuse_hi", [8192, 1024], BF16)
    fuse_lo = dint("fuse_lo", [8192, 1024], BF16)
    lat_in = dint("lat_in", [2048, 1024])
    lat_out = dint("lat_out", [4096, 1024])
    hid_hi = dint("hid_hi", [5632, 1024], BF16)
    hid_lo = dint("hid_lo", [5632, 1024], BF16)
    aout1 = dint("aout1", [1536, 2048])
    aout1_out = dint("aout1_out", [3072, 2048])

    def sb(name, shape, dtype):
        return nc.alloc_sbuf_tensor(name, shape, dtype).ap()

    sb_ident = sb("sb_ident", [128, 128], BF16)
    sb_cst = sb("sb_cst", [128, 4], F32)
    sb_ones = sb("sb_ones", [1, 128], F32)
    sb_onescol = sb("sb_onescol", [128, 1], F32)
    sb_bo = sb("sb_bo", [128, 16], F32)
    sb_sg = sb("sb_sg", [128, 64], F32)
    sb_su = sb("sb_su", [128, 64], F32)
    sb_sd = sb("sb_sd", [128, 16], F32)
    sb_maske = sb("sb_maske", [128, 256], BF16)
    sb_masko = sb("sb_masko", [128, 256], BF16)
    sb_cos = sb("sb_cos", [64, 2048], F32)
    sb_sin = sb("sb_sin", [64, 2048], F32)
    sb_thr = sb("sb_thr", [128, 8], F32)
    sb_gm = sb("sb_gm", [128, 8], F32)
    sb_gm2 = sb("sb_gm2", [128, 132], F32)
    sb_sm = sb("sb_sm", [128, 8], F32)
    off_sb = sb("off_sb", [1, 1], mybir.dt.int32)
    x_res = sb("x_res", [128, 16 * 1024], F32)
    xn_hi = sb("xn_hi", [128, 16 * 1024], BF16)
    xn_lo = sb("xn_lo", [128, 16 * 1024], BF16)
    K = 1024
    ARENA = (nc.sbuf_bytes_remaining - 384) // 64 * 64
    assert ARENA >= 57 * K, ARENA
    arena_t = nc.alloc_sbuf_tensor("arena", [128, ARENA], mybir.dt.int8)
    PH = None
    for a in nc.m.functions[0].allocations:
        if (isinstance(a, mybir.MemoryLocationSet) and a.memorylocations
                and a.memorylocations[0].name == arena_t.name):
            PH = a.memorylocations[0].addr
    assert PH is not None

    def addr_of(t):
        for a in nc.m.functions[0].allocations:
            if (isinstance(a, mybir.MemoryLocationSet) and a.memorylocations
                    and a.memorylocations[0].name == t.name):
                return a.memorylocations[0].addr
        raise KeyError(t.name)

    XO = addr_of(xn_hi.tensor)
    assert addr_of(xn_lo.tensor) == XO + 32768, "xn tiles not contiguous"

    def sb_at(name, shape, dtype, off):
        nbytes = int(np.prod(shape[1:])) * mybir.dt.size(dtype)
        inA = off >= PH and (off - PH) + nbytes <= ARENA
        inX = off >= XO and (off - XO) + nbytes <= 65536
        assert inA or inX, (name, off)
        return nc.alloc_sbuf_tensor_at(name, shape, dtype, offset=off).ap()

    scr = sb_at("scr", [128, 1024], F32, PH)
    scr2 = sb_at("scr2", [128, 1024], F32, PH + 4 * K)
    ab_b = sb_at("ab_b", [128, 1024], F32, PH + 8 * K)
    bb_b = sb_at("bb_b", [128, 1024], F32, PH + 12 * K)
    ln_t = sb_at("ln_t", [1, 4096], F32, PH + 16 * K)

    kv_wkh = sb_at("kv_wkh", [128, 4096], BF16, PH)
    kv_wkl = sb_at("kv_wkl", [128, 4096], BF16, PH + 8 * K)
    kv_wvh = sb_at("kv_wvh", [128, 4096], BF16, PH + 16 * K)
    kv_wvl = sb_at("kv_wvl", [128, 4096], BF16, PH + 24 * K)
    kv_wstg = [sb_at(f"kv_wstg{i}", [128, 2048], F32, PH + 32 * K + i * 8 * K)
               for i in range(2)]
    kq_hi = sb_at("kq_hi", [128, 256], BF16, PH + 56 * K + 512)
    kq_lo = sb_at("kq_lo", [128, 256], BF16, PH + 57 * K)
    vq_hi = sb_at("vq_hi", [128, 260], BF16, PH + 57 * K + 512)
    vq_lo = sb_at("vq_lo", [128, 260], BF16, PH + 58 * K + 64)
    kv_sb = sb_at("kv_sb", [64, 2080], F32, PH + 48 * K)

    q_wstg = [sb_at(f"q_wstg{i}", [128, 1024], F32, PH + i * 4 * K)
              for i in range(2)]
    q_wh = [sb_at(f"q_wh{i}", [128, 1024], BF16, PH + 8 * K + i * 4 * K)
            for i in range(2)]
    q_wl = [sb_at(f"q_wl{i}", [128, 1024], BF16, PH + 10 * K + i * 4 * K)
            for i in range(2)]
    q_hi = sb_at("q_hi", [64, 1024], BF16, PH + 16 * K)
    q_lo = sb_at("q_lo", [64, 1024], BF16, PH + 18 * K)
    r_sb = sb_at("r_sb", [1, 1024], F32, PH + 20 * K)
    rb_sb = sb_at("rb_sb", [64, 1024], F32, PH + 24 * K)
    oq_t = sb_at("oq_t", [64, 1024], F32, PH + 28 * K)
    out_hi = sb_at("out_hi", [64, 1024], BF16, PH + 32 * K)
    out_lo = sb_at("out_lo", [64, 1024], BF16, PH + 34 * K)
    kv_hi = sb_at("kv_hi", [64, 2080], BF16, PH + 36 * K)
    kv_lo = sb_at("kv_lo", [64, 2080], BF16, PH + 41 * K)

    wo_stg = [sb_at(f"wo_stg{i}", [128, 512], F32, PH + i * 2 * K)
              for i in range(2)]
    wo_hi = [sb_at(f"wo_hi{i}", [128, 512], BF16, PH + 4 * K + i * K)
             for i in range(2)]
    wo_lo = [sb_at(f"wo_lo{i}", [128, 512], BF16, PH + 6 * K + i * K)
             for i in range(2)]
    wo_rhi = [sb_at(f"wo_rhi{i}", [128, 1024], BF16, PH + 8 * K + i * 2 * K)
              for i in range(2)]
    wo_rlo = [sb_at(f"wo_rlo{i}", [128, 1024], BF16, PH + 12 * K + i * 2 * K)
              for i in range(2)]

    gu_wg = [sb_at(f"gu_wg{i}", [128, 4096], BF16, PH + i * 8 * K)
             for i in range(2)]
    gu_wu = [sb_at(f"gu_wu{i}", [128, 4096], BF16, PH + 16 * K + i * 8 * K)
             for i in range(2)]
    gu_g = sb_at("gu_g", [128, 1024], F32, PH + 32 * K)
    gu_u = sb_at("gu_u", [128, 1024], F32, PH + 36 * K)
    gu_f = sb_at("gu_f", [128, 1024], F32, PH + 40 * K)
    gu_fh = [sb_at(f"gu_fh{i}", [128, 1024], BF16, PH + 44 * K + i * 2 * K)
             for i in range(2)]
    gu_fl = [sb_at(f"gu_fl{i}", [128, 1024], BF16, PH + 48 * K + i * 2 * K)
             for i in range(2)]

    dn_w = [sb_at(f"dn_w{i}", [128, 4096], BF16, PH + 16 * K + i * 8 * K)
            for i in range(2)]
    dn_fh = [sb_at(f"dn_fh{i}", [128, 2048], BF16, PH + 32 * K + i * 4 * K)
             for i in range(2)]
    dn_fl = [sb_at(f"dn_fl{i}", [128, 2048], BF16, PH + 40 * K + i * 4 * K)
             for i in range(2)]

    lt_st = [sb_at(f"lt_st{i}", [128, 1024], F32, PH + 16 * K + i * 4 * K)
             for i in range(2)]
    gm_slab = [sb_at(f"gm_slab{i}", [128, 2048], F32, PH + 28 * K + i * 8 * K)
               for i in range(2)]
    dn3_stg = [sb_at(f"dn3_stg{i}", [128, 2048], F32, PH + i * 8 * K)
               for i in range(2)]
    w3_tmp2 = sb_at("w3_tmp2", [128, 2048], BF16, PH + 48 * K)
    xw1_stg = sb_at("xw1_stg", [128, 2048], F32, PH)
    xw2_stg = sb_at("xw2_stg", [128, 2048], F32, PH + 8 * K)
    xw_tmp = sb_at("xw_tmp", [128, 2048], BF16, PH + 16 * K)
    xw_tmp2 = sb_at("xw_tmp2", [128, 2048], BF16, PH + 20 * K)
    q1s = [sb_at(f"q1s{i}", [128, 2048], BF16, PH + 24 * K + i * 4 * K)
           for i in range(2)]
    q2s = [sb_at(f"q2s{i}", [128, 2048], BF16, PH + 32 * K + i * 4 * K)
           for i in range(2)]
    hd_s = sb_at("hd_s", [128, 1024], F32, PH + 40 * K)
    hd_f = sb_at("hd_f", [128, 1024], F32, PH + 44 * K)
    hd_h = [sb_at(f"hd_h{i}", [128, 1024], BF16, PH + 48 * K + i * 2 * K)
            for i in range(2)]
    hd_l = [sb_at(f"hd_l{i}", [128, 1024], BF16, PH + 52 * K + i * 2 * K)
            for i in range(2)]
    kvl_hi = sb_at("kvl_hi", [128, 4 * 2048], BF16, PH)
    kvl_lo = sb_at("kvl_lo", [128, 4 * 2048], BF16, PH + 16 * K)
    kvstg = sb_at("kvstg", [128, 2048], F32, PH + 32 * K)
    ql_stg = [sb_at(f"ql_stg{i}", [128, 1024], F32, PH + 32 * K + i * 8 * K)
              for i in range(2)]
    ql_hi = [sb_at(f"ql_hi{i}", [128, 1024], BF16, PH + 36 * K + i * 8 * K)
             for i in range(2)]
    ql_lo = [sb_at(f"ql_lo{i}", [128, 1024], BF16, PH + 38 * K + i * 8 * K)
             for i in range(2)]
    qw_stg = sb_at("qw_stg", [128, 1152], F32, PH + 48 * K)
    qw_hi = [sb_at(f"qw_hi{i}", [128, 1152], BF16, PH + 52 * K + 512 + i * 2304)
             for i in range(2)]
    qw_lo = [sb_at(f"qw_lo{i}", [128, 1152], BF16, XO + 48 * K + i * 2304)
             for i in range(2)]
    kn_hi = sb_at("kn_hi", [128, 2048], BF16, PH + 32 * K)
    kn_lo = sb_at("kn_lo", [128, 2048], BF16, PH + 36 * K)
    av_hi = sb_at("av_hi", [128, 3072], BF16, PH + 40 * K)
    av_lo = sb_at("av_lo", [128, 3072], BF16, PH + 46 * K)
    rsc1 = sb_at("rsc1", [64, 512], F32, XO + 58 * K)
    rsc2 = sb_at("rsc2", [64, 512], F32, XO + 60 * K)
    rsc3 = sb_at("rsc3", [64, 512], F32, XO + 62 * K)
    ptr_hi = sb_at("ptr_hi", [128, 512], BF16, PH + 56 * K)
    ptr_lo = sb_at("ptr_lo", [128, 512], BF16, PH + 57 * K)
    pv_st = sb_at("pv_st", [128, 512], F32, PH + 58 * K)
    qn_hi = sb_at("qn_hi", [128, 2048], BF16, XO)
    qn_lo = sb_at("qn_lo", [128, 2048], BF16, XO + 4 * K)
    qr_hi = sb_at("qr_hi", [64, 2048], BF16, XO + 8 * K)
    qr_lo = sb_at("qr_lo", [64, 2048], BF16, XO + 12 * K)
    kr_hi = sb_at("kr_hi", [64, 2048], BF16, XO + 16 * K)
    kr_lo = sb_at("kr_lo", [64, 2048], BF16, XO + 20 * K)
    p_f = sb_at("p_f", [128, 2048], F32, XO + 24 * K)
    ph_p = sb_at("ph_p", [128, 4096], BF16, XO + 32 * K)
    pl_p = sb_at("pl_p", [128, 4096], BF16, XO + 40 * K)
    kw_stg = sb_at("kw_stg", [128, 768], F32, XO + 52 * K)
    kw_hi = sb_at("kw_hi", [128, 768], BF16, XO + 55 * K)
    kw_lo = sb_at("kw_lo", [128, 768], BF16, XO + 56 * K + 512)
    vw_stg = sb_at("vw_stg", [128, 768], F32, XO + 52 * K)
    vw_hi = sb_at("vw_hi", [128, 768], BF16, XO + 55 * K)
    vw_lo = sb_at("vw_lo", [128, 768], BF16, XO + 56 * K + 512)
    w1r_stg = [sb_at(f"w1r_stg{i}", [128, 1024], F32, PH + 16 * K + i * 4 * K)
               for i in range(2)]
    w1r_hi = [sb_at(f"w1r_hi{i}", [128, 1024], BF16, PH + 24 * K + i * 2 * K)
              for i in range(2)]
    w1r_lo = [sb_at(f"w1r_lo{i}", [128, 1024], BF16, PH + 28 * K + i * 2 * K)
              for i in range(2)]

    psA_t = nc.alloc_psum_tensor("psA", [128, 2048], F32)
    psA = psA_t.ap()
    pbankB = [nc.alloc_psum_tensor(f"pb{i}", [128, 512], F32) for i in range(4, 8)]
    pbank = [psA[:, i * 512:(i + 1) * 512] for i in range(4)] + \
            [t.ap() for t in pbankB]
    pt_ps = pbankB[2].bitcast(BF16).ap()
    pt_ps2 = pbankB[3].bitcast(BF16).ap()

    s_pro = p.sem("s_pro")
    s_mm = p.sem("s_mm")
    s_v = p.sem("s_v")
    s_s = p.sem("s_s")
    s_free = p.sem("s_free")
    s_cc = p.sem("s_cc")
    s_st = p.sem("s_st")
    s_kwh = p.sem("s_kwh")
    s_kstg = p.sem("s_kstg")
    s_kvin = p.sem("s_kvin")
    s_kvout = p.sem("s_kvout")
    s_guw = [p.sem("s_guw0"), p.sem("s_guw1")]
    s_fu = [p.sem("s_fu0"), p.sem("s_fu1")]
    s_qwa = [p.sem("s_qwa0"), p.sem("s_qwa1")]
    s_qwb = [p.sem("s_qwb0"), p.sem("s_qwb1")]
    s_ao = p.sem("s_ao")
    s_woa = [p.sem("s_woa0"), p.sem("s_woa1")]
    s_wob = [p.sem("s_wob0"), p.sem("s_wob1")]
    s_dn = [p.sem("s_dn0"), p.sem("s_dn1")]
    s_dbg = p.sem("s_dbg")
    s_lt = [p.sem("s_lt0"), p.sem("s_lt1")]
    s_ltb = [p.sem("s_ltb0"), p.sem("s_ltb1")]
    s_lte = [p.sem("s_lte0"), p.sem("s_lte1")]
    s_gm = [p.sem("s_gm0"), p.sem("s_gm1")]
    s_w3s = [p.sem("s_w3s0"), p.sem("s_w3s1")]
    s_x1w = [p.sem("s_x1w0"), p.sem("s_x1w1")]
    s_hid = [p.sem("s_hid0"), p.sem("s_hid1")]
    s_kvla = p.sem("s_kvla")
    s_kvlb = p.sem("s_kvlb")
    s_qla = p.sem("s_qla")
    s_qla2 = [p.sem("s_qla2_0"), p.sem("s_qla2_1")]
    s_qlb2 = [p.sem("s_qlb2_0"), p.sem("s_qlb2_1")]
    s_qwsa = [p.sem("s_qwsa0"), p.sem("s_qwsa1")]
    s_qwsb = [p.sem("s_qwsb0"), p.sem("s_qwsb1")]
    s_kw = p.sem("s_kw")
    s_kwb = p.sem("s_kwb")
    s_vw = p.sem("s_vw")
    s_vwb = p.sem("s_vwb")
    s_av = p.sem("s_av")
    s_dn1 = [p.sem("s_dn1_0"), p.sem("s_dn1_1")]
    s_w1r = [p.sem("s_w1r0"), p.sem("s_w1r1")]

    q_, t_, v_, s_, g_ = p.q, p.t, p.v, p.s, p.g

    for (dstt, srct) in ((sb_ident, ident_bf), (sb_ones, ones_f), (sb_bo, bo_t),
                         (sb_sg, sg_t), (sb_su, su_t), (sb_sd, sd_t),
                         (sb_maske, mask_e), (sb_masko, mask_o),
                         (sb_cos, cosd), (sb_sin, sind)):
        q_.dma_start(out=dstt[:, :], in_=srct[:, :], inc=dinc(s_pro))
    s_.dma_start(out=sb3(x_res[:, :], 1024), in_=chunked(x_in), inc=dinc(s_pro))
    v_.memset(sb_onescol[:, :], 1.0)
    v_.memset(sb_cst[:, 0:1], 1e-5)
    v_.memset(vq_hi.rearrange("p (h c) -> p h c", c=65)[:, :, 64:65], 1.0)
    v_.memset(vq_lo.rearrange("p (h c) -> p h c", c=65)[:, :, 64:65], 0.0,
              inc=inc1(s_v))

    def layernorm(src_tile, dst_hi, dst_lo):
        t_.wait(s_pro)
        t_.wait(s_v)
        s_.wait(s_pro)
        s_.wait(s_v)
        mm_base = s_mm.n
        ss_base = s_s.n
        for ic in range(NCH):
            buf = scr if ic % 2 == 0 else scr2
            if ic >= 2:
                s_.wait(s_mm, mm_base + (ic - 1))
            s_.activation(out=buf[:, :], in_=src_tile[:, ts(ic, 1024)],
                          func=AF.Square, inc=inc1(s_s))
            t_.wait(s_s, ss_base + ic + 1)
            st = (ic == 0); sp = (ic == NCH - 1)
            for tb in range(2):
                xs = src_tile[:, ic * 1024 + tb * 512: ic * 1024 + (tb + 1) * 512]
                t_.matmul(pbank[4 + tb][0:1, :], sb_onescol[:, :], xs,
                          start=st, stop=sp)
                t_.matmul(pbank[6 + tb][0:1, :], sb_onescol[:, :], buf[:, ts(tb, 512)],
                          start=st, stop=sp,
                          inc=inc1(s_mm) if tb == 1 else None)
        m = ln_t[0:1, 0:1024]; msq = ln_t[0:1, 1024:2048]
        va = ln_t[0:1, 2048:3072]; rinv = ln_t[0:1, 3072:4096]
        v_.wait(s_mm)
        for tb in range(2):
            v_.tensor_scalar_mul(m[:, ts(tb, 512)], pbank[4 + tb][0:1, :],
                                 1.0 / 2048.0, inc=inc1(s_v) if tb == 1 else None)
        s_.wait(s_v)
        s_.activation(out=msq, in_=m, func=AF.Square, inc=inc1(s_s))
        v_.wait(s_s)
        for tb in range(2):
            v_.scalar_tensor_tensor(va[:, ts(tb, 512)], pbank[6 + tb][0:1, :],
                                    1.0 / 2048.0, msq[:, ts(tb, 512)],
                                    OP.mult, OP.subtract,
                                    inc=inc1(s_v) if tb == 1 else None)
        s_.wait(s_v)
        s_.activation(out=rinv, in_=va, func=AF.Sqrt, bias=sb_cst[0:1, 0:1],
                      inc=inc1(s_s))
        v_.wait(s_s)
        v_.reciprocal(rinv, rinv)
        v_.drain()
        v_.tensor_tensor(va, m, rinv, OP.mult)
        v_.drain()
        v_.tensor_scalar_mul(va, va, -1.0, inc=inc1(s_v))
        t_.wait(s_v)
        for tb in range(2):
            t_.matmul(pbank[0 + tb][:, :], sb_ones[:, :], rinv[:, ts(tb, 512)],
                      start=True, stop=True)
            t_.matmul(pbank[2 + tb][:, :], sb_ones[:, :], va[:, ts(tb, 512)],
                      start=True, stop=True,
                      inc=inc1(s_mm) if tb == 1 else None)
        v_.wait(s_mm)
        for tb in range(2):
            v_.tensor_copy(out=ab_b[:, ts(tb, 512)], in_=pbank[0 + tb][:, :])
            v_.tensor_copy(out=bb_b[:, ts(tb, 512)], in_=pbank[2 + tb][:, :])
        v_.drain()
        for ic in range(NCH):
            v_.tensor_tensor(scr[:, :], src_tile[:, ts(ic, 1024)], ab_b[:, :], OP.mult)
            v_.drain()
            v_.tensor_tensor(scr2[:, :], scr[:, :], bb_b[:, :], OP.add, inc=inc1(s_v))
            s_.wait(s_v)
            s_.copy(out=dst_hi[:, ts(ic, 1024)], in_=scr2[:, :], inc=inc1(s_s))
            v_.wait(s_s)
            v_.tensor_tensor(dst_lo[:, ts(ic, 1024)], scr2[:, :],
                             dst_hi[:, ts(ic, 1024)], OP.subtract, inc=inc1(s_v))

    mark('LN0')
    layernorm(x_res, xn_hi, xn_lo)

    def xn_piece(hi, ic, tkslice):
        src = xn_hi if hi else xn_lo
        return src[:, ic * 1024 + tkslice.start: ic * 1024 + tkslice.stop]

    mark('kv')
    if stage >= 1:
        kv_stg_marks = {}
        for e8 in range(8):
            rsl = slice(e8 * 128, (e8 + 1) * 128)
            g_.wait(s_mm)
            g_.wait(s_v)
            g_.dma_start(out=kv_wkh[:, :], in_=wkp[rsl, :], inc=dinc(s_kwh))
            g_.dma_start(out=kv_wvh[:, :], in_=wvp[rsl, :], inc=dinc(s_kwh))
            kwh_done = s_kwh.n
            kfills = [(whi, wlo, hf2)
                      for (whi, wlo) in ((kv_wkh, kv_wkl), (kv_wvh, kv_wvl))
                      for hf2 in range(2)]
            for fi, (whi, wlo, hf2) in enumerate(kfills):
                sbuf_ = fi % 2
                wsrc = wkp if whi is kv_wkh else wvp
                if sbuf_ in kv_stg_marks:
                    q_.wait(s_v, kv_stg_marks[sbuf_])
                q_.dma_start(
                    out=kv_wstg[sbuf_][:, :],
                    in_=wsrc[rsl, hf2 * 2048:(hf2 + 1) * 2048],
                    inc=dinc(s_kstg) if sbuf_ == 0 else dinc(s_kvin))
                v_.wait(s_kstg if sbuf_ == 0 else s_kvin)
                v_.wait(s_kwh, kwh_done)
                v_.tensor_tensor(wlo[:, hf2 * 2048:(hf2 + 1) * 2048],
                                 kv_wstg[sbuf_][:, :],
                                 whi[:, hf2 * 2048:(hf2 + 1) * 2048],
                                 OP.subtract, inc=inc1(s_v))
                kv_stg_marks[sbuf_] = s_v.n
            t_.wait(s_v)
            for tc in range(8):
                tk = slice(tc * 128, (tc + 1) * 128)
                for kc in range(NCH):
                    wkh = kv_wkh[:, ts(kc, 256)]; wkl = kv_wkl[:, ts(kc, 256)]
                    wvh = kv_wvh[:, ts(kc, 256)]; wvl = kv_wvl[:, ts(kc, 256)]
                    xh = xn_piece(True, kc, tk); xl = xn_piece(False, kc, tk)
                    st = (kc == 0); sp = (kc == NCH - 1)
                    t_.matmul(pbank[0][:, 0:256], xh, wkh, start=st, stop=False)
                    t_.matmul(pbank[0][:, 0:256], xh, wkl, start=False, stop=False)
                    t_.matmul(pbank[1][:, 0:256], xh, wvh, start=st, stop=False)
                    t_.matmul(pbank[1][:, 0:256], xh, wvl, start=False, stop=False)
                    t_.matmul(pbank[0][:, 0:256], xl, wkh, start=False, stop=sp)
                    t_.matmul(pbank[1][:, 0:256], xl, wvh, start=False, stop=sp,
                              inc=inc1(s_mm) if sp else None)
                s_.wait(s_mm)
                s_.activation(out=kq_hi[:, :], in_=pbank[0][:, 0:256], func=AF.Relu,
                              inc=inc1(s_s))
                v_.wait(s_s)
                v_.scalar_tensor_tensor(kq_lo[:, :], pbank[0][:, 0:256], 0.0,
                                        kq_hi[:, :], OP.max, OP.subtract)
                vd_h = vq_hi.rearrange("p (h c) -> p h c", c=65)[:, :, 0:64]
                vd_l = vq_lo.rearrange("p (h c) -> p h c", c=65)[:, :, 0:64]
                s_.copy(out=vd_h,
                        in_=pbank[1][:, 0:256].rearrange("p (h c) -> p h c", c=64),
                        inc=inc1(s_s))
                v_.wait(s_s)
                v_.tensor_tensor(vd_l,
                                 pbank[1][:, 0:256].rearrange("p (h c) -> p h c", c=64),
                                 vd_h, OP.subtract, inc=inc1(s_v))
                t_.wait(s_v)
                for hh in range(4):
                    kh = kq_hi[:, ts(hh, 64)]; kl = kq_lo[:, ts(hh, 64)]
                    vh = vq_hi[:, hh * 65:(hh + 1) * 65]
                    vl = vq_lo[:, hh * 65:(hh + 1) * 65]
                    od = pbank[2][0:64, hh * 65:(hh + 1) * 65]
                    t_.matmul(od, kh, vh, start=(tc == 0 and hh == 0), stop=False)
                    t_.matmul(od, kh, vl, start=False, stop=False)
                    t_.matmul(od, kl, vh, start=False,
                              stop=(tc == 7 and hh == 3),
                              inc=inc1(s_mm) if (tc == 7 and hh == 3) else None)
            v_.wait(s_mm)
            v_.tensor_copy(out=kv_sb[:, e8 * 260:(e8 + 1) * 260],
                           in_=pbank[2][0:64, 0:260], inc=inc1(s_v))
        q_.wait(s_v)
        q_.dma_start(out=kvred_in[:, :], in_=kv_sb[:, :], inc=dinc(s_kvin))
        g_.wait(s_kvin)
        g_.collective_compute(
            "AllReduce", OP.add,
            replica_groups=[[0, 1], [2, 3], [4, 5], [6, 7]],
            ins=[kvred_in[:, :]], outs=[kvred_out[:, :]], inc=inc1(s_cc))

    mark('gateup')
    fuse_done = [0, 0]
    fu_marks = {}
    if stage >= 2:
        mm_marks = []
        for mg in range(32):
            buf = mg % 2
            if mg >= 2:
                g_.wait(s_mm, mm_marks[mg - 2])
            else:
                g_.wait(s_v)
                g_.wait(s_mm)
            g_.dma_start(out=gu_wg[buf][:, :], in_=wg8p[ts(mg, 128), :],
                         inc=dinc(s_guw[buf]))
            g_.dma_start(out=gu_wu[buf][:, :], in_=wu8p[ts(mg, 128), :],
                         inc=dinc(s_guw[buf]))
            t_.wait(s_guw[buf])
            t_.wait(s_s)
            for kc in range(NCH):
                st = (kc == 0); sp = (kc == NCH - 1)
                for mc in range(2):
                    wgt = gu_wg[buf][:, kc * 256 + mc * 128: kc * 256 + (mc + 1) * 128]
                    wut = gu_wu[buf][:, kc * 256 + mc * 128: kc * 256 + (mc + 1) * 128]
                    for tb in range(2):
                        xh = xn_piece(True, kc, ts(tb, 512))
                        xl = xn_piece(False, kc, ts(tb, 512))
                        pg = pbank[mc * 2 + tb]
                        pu = pbank[4 + mc * 2 + tb]
                        t_.matmul(pg[:, :], wgt, xh, start=st, stop=False)
                        t_.matmul(pg[:, :], wgt, xl, start=False, stop=sp)
                        t_.matmul(pu[:, :], wut, xh, start=st, stop=False)
                        t_.matmul(pu[:, :], wut, xl, start=False, stop=sp,
                                  inc=inc1(s_mm)
                                  if (sp and mc == 1 and tb == 1) else None)
            mm_marks.append(s_mm.n)
            s_.wait(s_mm)
            for mc in range(2):
                mabs = mg * 2 + mc
                fb = mabs % 2
                sga = sb_sg[:, mabs:mabs + 1]
                sua = sb_su[:, mabs:mabs + 1]
                s_.wait(s_v)
                for tb in range(2):
                    s_.activation(out=gu_g[:, ts(tb, 512)], in_=pbank[mc * 2 + tb][:, :],
                                  func=AF.Silu, scale=sga)
                    s_.activation(out=gu_u[:, ts(tb, 512)], in_=pbank[4 + mc * 2 + tb][:, :],
                                  func=AF.Copy, scale=sua, inc=inc1(s_s))
                v_.wait(s_s)
                v_.tensor_tensor(gu_f[:, :], gu_g[:, :], gu_u[:, :], OP.mult)
                if fuse_done[fb]:
                    v_.wait(s_fu[fb], fuse_done[fb])
                v_.drain()
                v_.tensor_copy(out=gu_fh[fb][:, :], in_=gu_f[:, :])
                v_.drain()
                v_.tensor_tensor(gu_fl[fb][:, :], gu_f[:, :], gu_fh[fb][:, :],
                                 OP.subtract, inc=inc1(s_v))
                q_.wait(s_v)
                q_.dma_start(out=fuse_hi[ts(mabs, 128), :], in_=gu_fh[fb][:, :],
                             inc=dinc(s_fu[fb]))
                q_.dma_start(out=fuse_lo[ts(mabs, 128), :], in_=gu_fl[fb][:, :],
                             inc=dinc(s_fu[fb]))
                fuse_done[fb] = s_fu[fb].n
                fu_marks[mabs] = s_fu[fb].n

    mark('qapply')
    aout_done = 0
    if stage >= 1:
        g_.wait(s_cc)
        g_.wait(s_mm)
        g_.wait(s_v)
        if fuse_done[0]:
            g_.wait(s_fu[0], fuse_done[0])
            g_.wait(s_fu[1], fuse_done[1])
        g_.dma_start(out=kv_sb[:, :], in_=kvred_out[:, :], inc=dinc(s_kvout))
        s_.wait(s_kvout)
        s_.copy(out=kv_hi[:, :], in_=kv_sb[:, :], inc=inc1(s_s))
        v_.wait(s_s)
        v_.tensor_tensor(kv_lo[:, :], kv_sb[:, :], kv_hi[:, :], OP.subtract,
                         inc=inc1(s_v))
        mm_marks = []
        lo_marks = []

        def qfill(h):
            buf = h % 2
            if h >= 2:
                q_.wait(s_mm, mm_marks[h - 2])
                g_.wait(s_mm, mm_marks[h - 2])
            else:
                q_.wait(s_mm)
                g_.wait(s_mm)
            q_.dma_start(out=q_wstg[buf][:, :], in_=wqp[ts(h, 128), :],
                         inc=dinc(s_qwa[buf]))
            g_.dma_start(out=q_wh[buf][:, :], in_=wqp[ts(h, 128), :],
                         inc=dinc(s_qwb[buf]))
            v_.wait(s_qwa[buf])
            v_.wait(s_qwb[buf])
            v_.drain()
            v_.tensor_tensor(q_wl[buf][:, :], q_wstg[buf][:, :], q_wh[buf][:, :],
                             OP.subtract, inc=inc1(s_v))
            lo_marks.append(s_v.n)

        qfill(0)
        prev_qsplit = 0
        for h in range(32):
            buf = h % 2
            t_.wait(s_v, max(lo_marks[h], prev_qsplit))
            for kc in range(NCH):
                st = (kc == 0); sp = (kc == NCH - 1)
                wht = q_wh[buf][:, ts(kc, 64)]; wlt = q_wl[buf][:, ts(kc, 64)]
                for tb in range(2):
                    xh = xn_piece(True, kc, ts(tb, 512))
                    xl = xn_piece(False, kc, ts(tb, 512))
                    pq = pbank[tb][0:64, :]
                    t_.matmul(pq, wht, xh, start=st, stop=False)
                    t_.matmul(pq, wht, xl, start=False, stop=False)
                for tb in range(2):
                    xh = xn_piece(True, kc, ts(tb, 512))
                    pq = pbank[tb][0:64, :]
                    t_.matmul(pq, wlt, xh, start=False, stop=sp,
                              inc=inc1(s_mm)
                              if (sp and tb == 1) else None)
            mm_marks.append(s_mm.n)
            if h + 1 < 32:
                qfill(h + 1)
            s_.wait(s_mm, mm_marks[h])
            for tb in range(2):
                s_.activation(out=q_hi[:, ts(tb, 512)], in_=pbank[tb][0:64, :],
                              func=AF.Relu, inc=inc1(s_s) if tb == 1 else None)
            v_.wait(s_s)
            for tb in range(2):
                v_.scalar_tensor_tensor(q_lo[:, ts(tb, 512)], pbank[tb][0:64, :], 0.0,
                                        q_hi[:, ts(tb, 512)], OP.max, OP.subtract,
                                        inc=inc1(s_v) if tb == 1 else None)
            qsplit_mark = s_v.n
            prev_qsplit = qsplit_mark
            t_.wait(s_v, qsplit_mark)
            kvh = kv_hi[:, h * 65:h * 65 + 64]; kvl = kv_lo[:, h * 65:h * 65 + 64]
            ksh = kv_hi[:, h * 65 + 64:h * 65 + 65]
            ksl = kv_lo[:, h * 65 + 64:h * 65 + 65]
            for tb in range(2):
                qh = q_hi[:, ts(tb, 512)]; ql = q_lo[:, ts(tb, 512)]
                po = pbank[2 + tb][0:64, :]
                t_.matmul(po, kvh, qh, start=True, stop=False)
                t_.matmul(po, kvh, ql, start=False, stop=False)
                t_.matmul(po, kvl, qh, start=False, stop=True)
                pn = pbank[4 + tb][0:1, :]
                t_.matmul(pn, ksh, qh, start=True, stop=False)
                t_.matmul(pn, ksh, ql, start=False, stop=False)
                t_.matmul(pn, ksl, qh, start=False, stop=True,
                          inc=inc1(s_mm) if tb == 1 else None)
            qkv_mark = s_mm.n
            v_.wait(s_mm, qkv_mark)
            for tb in range(2):
                v_.tensor_scalar_max(r_sb[:, ts(tb, 512)], pbank[4 + tb][0:1, :],
                                     1e-6)
            v_.drain()
            v_.reciprocal(r_sb[:, :], r_sb[:, :], inc=inc1(s_v))
            t_.wait(s_v)
            for tb in range(2):
                t_.matmul(pbank[6 + tb][0:64, :], sb_ones[:, 0:64],
                          r_sb[:, ts(tb, 512)], start=True, stop=True,
                          inc=inc1(s_mm) if tb == 1 else None)
            v_.wait(s_mm)
            for tb in range(2):
                v_.tensor_copy(out=rb_sb[:, ts(tb, 512)], in_=pbank[6 + tb][0:64, :])
            v_.drain()
            for tb in range(2):
                v_.tensor_tensor(oq_t[:, ts(tb, 512)], pbank[2 + tb][0:64, :],
                                 rb_sb[:, ts(tb, 512)], OP.mult,
                                 inc=inc1(s_v) if tb == 1 else None)
            s_.wait(s_v)
            if h >= 1:
                s_.wait(s_ao, aout_done)
            s_.copy(out=out_hi[:, :], in_=oq_t[:, :], inc=inc1(s_s))
            v_.wait(s_s)
            if h >= 1:
                v_.wait(s_ao, aout_done)
            v_.tensor_tensor(out_lo[:, :], oq_t[:, :], out_hi[:, :], OP.subtract,
                             inc=inc1(s_v))
            q_.wait(s_v)
            q_.dma_start(out=aout0_hi[ts(h, 64), :], in_=out_hi[:, :],
                         inc=dinc(s_ao))
            q_.dma_start(out=aout0_lo[ts(h, 64), :], in_=out_lo[:, :],
                         inc=dinc(s_ao))
            aout_done = s_ao.n

    pre_wo0_mm = s_mm.n
    pre_wo0_v = s_v.n
    pre_wo0_ao = aout_done
    mark('wo0')
    if stage >= 3:
        mm_marks = []
        u = 0
        for mcg in range(4):
            t_.wait(s_v)
            for kc in range(NCH):
                buf = u % 2
                for eng in (q_, g_):
                    if u >= 2:
                        eng.wait(s_mm, mm_marks[u - 2])
                    else:
                        eng.wait(s_mm)
                        eng.wait(s_v)
                        eng.wait(s_ao, aout_done)
                q_.dma_start(out=wo_stg[buf][:, :],
                             in_=wo0[ts(kc, 128), ts(mcg, 512)], inc=dinc(s_woa[buf]))
                g_.dma_start(out=wo_hi[buf][:, :],
                             in_=wo0[ts(kc, 128), ts(mcg, 512)], inc=dinc(s_wob[buf]))
                g_.dma_start(out=wo_rhi[buf][:, :], in_=aout0_hi[ts(kc, 128), :],
                             inc=dinc(s_wob[buf]))
                g_.dma_start(out=wo_rlo[buf][:, :], in_=aout0_lo[ts(kc, 128), :],
                             inc=dinc(s_wob[buf]))
                v_.wait(s_woa[buf])
                v_.wait(s_wob[buf])
                v_.tensor_tensor(wo_lo[buf][:, :], wo_stg[buf][:, :],
                                 wo_hi[buf][:, :], OP.subtract, inc=inc1(s_v))
                t_.wait(s_v)
                st = (kc == 0); sp = (kc == NCH - 1)
                for mc in range(4):
                    wht = wo_hi[buf][:, ts(mc, 128)]; wlt = wo_lo[buf][:, ts(mc, 128)]
                    for tb in range(2):
                        rh = wo_rhi[buf][:, ts(tb, 512)]
                        rl = wo_rlo[buf][:, ts(tb, 512)]
                        po = pbank[mc * 2 + tb]
                        t_.matmul(po[:, :], wht, rh, start=st, stop=False)
                        t_.matmul(po[:, :], wht, rl, start=False, stop=False)
                    for tb in range(2):
                        rh = wo_rhi[buf][:, ts(tb, 512)]
                        po = pbank[mc * 2 + tb]
                        t_.matmul(po[:, :], wlt, rh, start=False, stop=sp,
                                  inc=inc1(s_mm)
                                  if (mc == 3 and tb == 1) else None)
                mm_marks.append(s_mm.n)
                u += 1
            v_.wait(s_mm)
            for mc in range(4):
                mabs = mcg * 4 + mc
                for tb in range(2):
                    xd = x_res[:, mabs * 1024 + tb * 512: mabs * 1024 + (tb + 1) * 512]
                    v_.scalar_tensor_tensor(xd, pbank[mc * 2 + tb][:, :],
                                            sb_bo[:, mabs:mabs + 1], xd,
                                            OP.add, OP.add,
                                            inc=inc1(s_v) if (mc == 3 and tb == 1) else None)

    pre_gm_v = s_v.n

    mark('down0')
    if stage >= 3:
        mm_marks = []
        ep_marks = []
        u = 0
        for tb in range(2):
            for mcg in range(2):
                t_.wait(s_v, ep_marks[-1] if ep_marks else pre_gm_v)
                for kcq in range(16):
                    buf = u % 2
                    if u >= 2:
                        g_.wait(s_mm, mm_marks[u - 2])
                    else:
                        g_.wait(s_mm, pre_wo0_mm)
                        g_.wait(s_v, pre_wo0_v)
                        g_.wait(s_ao, pre_wo0_ao)
                    g_.dma_start(out=dn_w[buf][:, :],
                                 in_=wdp[(mcg * 16 + kcq) * 128:
                                         (mcg * 16 + kcq + 1) * 128, :],
                                 inc=dinc(s_dn[buf]))
                    need_fu = [0, 0]
                    for kc4 in range(4):
                        kc = kcq * 4 + kc4
                        need_fu[kc % 2] = max(need_fu[kc % 2], fu_marks[kc])
                    g_.wait(s_fu[0], need_fu[0])
                    g_.wait(s_fu[1], need_fu[1])
                    for kc4 in range(4):
                        kc = kcq * 4 + kc4
                        g_.dma_start(out=dn_fh[buf][:, ts(kc4, 512)],
                                     in_=fuse_hi[ts(kc, 128), ts(tb, 512)],
                                     inc=dinc(s_dn[buf]))
                        g_.dma_start(out=dn_fl[buf][:, ts(kc4, 512)],
                                     in_=fuse_lo[ts(kc, 128), ts(tb, 512)],
                                     inc=dinc(s_dn[buf]))
                    t_.wait(s_dn[buf])
                    for kc4 in range(4):
                        st = (kcq == 0 and kc4 == 0); sp = (kcq == 15 and kc4 == 3)
                        for mc in range(8):
                            wt = dn_w[buf][:, kc4 * 1024 + mc * 128:
                                           kc4 * 1024 + (mc + 1) * 128]
                            po = pbank[mc]
                            t_.matmul(po[:, :], wt, dn_fh[buf][:, ts(kc4, 512)],
                                      start=st, stop=False)
                            t_.matmul(po[:, :], wt, dn_fl[buf][:, ts(kc4, 512)],
                                      start=False, stop=sp,
                                      inc=inc1(s_mm)
                                      if (kc4 == 3 and mc == 7) else None)
                    mm_marks.append(s_mm.n)
                    u += 1
                v_.wait(s_mm)
                for mc in range(8):
                    mabs = mcg * 8 + mc
                    xd = x_res[:, mabs * 1024 + tb * 512: mabs * 1024 + (tb + 1) * 512]
                    v_.scalar_tensor_tensor(xd, pbank[mc][:, :],
                                            sb_sd[:, mabs:mabs + 1], xd,
                                            OP.mult, OP.add,
                                            inc=inc1(s_v) if mc == 7 else None)
                ep_marks.append(s_v.n)

    CATT = 1.0 / float(np.sqrt(192.0))
    RG = [[0, 1], [2, 3], [4, 5], [6, 7]]
    tok_cell = {}
    if stage >= 4:
        q_.wait(s_pro)
        q_.dma_start(out=off_sb[:, :], in_=tok_off[:, :], inc=dinc(s_qla))

        def _load_tok(eng):
            eng.wait_ge(s_qla.h, 16)
            tok_cell["v"] = eng.value_load(off_sb[0:1, 0:1])
        q_.ops.append(_load_tok)

        mark('LN1')
        s_.drain()
        layernorm(x_res, xn_hi, xn_lo)

        gm_srcs = [(wi, j) for wi in range(3) for j in range(44)]
        gm_w = (w1p, w2p, w3p)
        s_.wait(s_v)
        s_.wait(s_mm)

        def gm_fill(idx):
            wi, j = gm_srcs[idx]
            s_.dma_start(out=gm_slab[idx % 2][:, :],
                         in_=gm_w[wi][j * 128:(j + 1) * 128, :],
                         inc=dinc(s_gm[idx % 2]))

        gm_fill(0)
        for idx in range(len(gm_srcs)):
            wv = s_gm[idx % 2].n
            if idx + 1 < len(gm_srcs):
                if idx >= 1:
                    s_.wait(s_s)
                gm_fill(idx + 1)
            wi, j = gm_srcs[idx]
            s_.wait(s_gm[idx % 2], wv)
            s_.activation(out=gm_slab[idx % 2][:, :], in_=gm_slab[idx % 2][:, :],
                          func=AF.Abs,
                          accum_out=sb_gm2[:, wi * 44 + j:wi * 44 + j + 1],
                          inc=inc1(s_s))

        mark('latents')
        lt_marks = []
        lte_done = [0, 0]
        u = 0
        for grp in range(4):
            wsrc = wqd if grp < 3 else wkvd
            wcol0 = grp * 512 if grp < 3 else 0
            for kc in range(NCH):
                buf = u % 2
                for eng in (q_, g_):
                    if u >= 2:
                        eng.wait(s_mm, lt_marks[u - 2])
                    else:
                        eng.wait(s_mm)
                        eng.wait(s_v)
                q_.dma_start(out=wo_stg[buf][:, :],
                             in_=wsrc[ts(kc, 128), wcol0:wcol0 + 512],
                             inc=dinc(s_lt[buf]))
                g_.dma_start(out=wo_hi[buf][:, :],
                             in_=wsrc[ts(kc, 128), wcol0:wcol0 + 512],
                             inc=dinc(s_ltb[buf]))
                v_.wait(s_lt[buf])
                v_.wait(s_ltb[buf])
                v_.tensor_tensor(wo_lo[buf][:, :], wo_stg[buf][:, :],
                                 wo_hi[buf][:, :], OP.subtract, inc=inc1(s_v))
                t_.wait(s_v)
                st = (kc == 0); sp = (kc == NCH - 1)
                for mc in range(4):
                    wht = wo_hi[buf][:, ts(mc, 128)]; wlt = wo_lo[buf][:, ts(mc, 128)]
                    for tb in range(2):
                        xh = xn_piece(True, kc, ts(tb, 512))
                        xl = xn_piece(False, kc, ts(tb, 512))
                        po = pbank[mc * 2 + tb]
                        t_.matmul(po[:, :], wht, xh, start=st, stop=False)
                        t_.matmul(po[:, :], wht, xl, start=False, stop=False)
                    for tb in range(2):
                        xh = xn_piece(True, kc, ts(tb, 512))
                        po = pbank[mc * 2 + tb]
                        t_.matmul(po[:, :], wlt, xh, start=False, stop=sp,
                                  inc=inc1(s_mm) if (mc == 3 and tb == 1) else None)
                lt_marks.append(s_mm.n)
                u += 1
            v_.wait(s_mm)
            for mc in range(4):
                eb = mc % 2
                if lte_done[eb]:
                    v_.wait(s_lte[eb], lte_done[eb])
                for tb in range(2):
                    v_.tensor_copy(out=lt_st[eb][:, ts(tb, 512)],
                                   in_=pbank[mc * 2 + tb][:, :],
                                   inc=inc1(s_v) if tb == 1 else None)
                q_.wait(s_v)
                row0 = grp * 512 + mc * 128
                q_.dma_start(out=lat_in[row0:row0 + 128, :], in_=lt_st[eb][:, :],
                             inc=dinc(s_lte[eb]))
                lte_done[eb] = s_lte[eb].n
        g_.wait(s_lte[0], lte_done[0])
        g_.wait(s_lte[1], lte_done[1])
        g_.collective_compute(
            "AllGather", OP.bypass, replica_groups=RG,
            ins=[lat_in[:, :]], outs=[lat_out[:, :]], inc=inc1(s_cc))
        ag1_done = s_cc.n

        mark('gamma')
        NW3 = 5632 * 2048
        for wi in range(3):
            v_.wait(s_s)
            v_.drain()
            v_.reduce_sum(sb_gm[:, wi:wi + 1], sb_gm2[:, wi * 44:(wi + 1) * 44],
                          axis=mybir.AxisListType.X, inc=inc1(s_v))
            t_.wait(s_v)
            t_.matmul(pbank[7][0:1, 0:1], sb_onescol[:, :], sb_gm[:, wi:wi + 1],
                      start=True, stop=True, inc=inc1(s_mm))
            v_.wait(s_mm)
            gv = sb_sm[0:1, 4:5]
            gvn = sb_sm[0:1, 5:6]
            v_.tensor_scalar(gv, pbank[7][0:1, 0:1], 0.5 / NW3, 0.5e-8,
                             OP.mult, OP.add)
            v_.drain()
            v_.tensor_scalar_mul(gvn, gv, -1.0, inc=inc1(s_v))
            t_.wait(s_v)
            t_.matmul(pbank[7][:, 0:1], sb_ones[:, :], gv, start=True, stop=True)
            t_.matmul(pbank[7][:, 1:2], sb_ones[:, :], gvn,
                      start=True, stop=True, inc=inc1(s_mm))
            v_.wait(s_mm)
            v_.tensor_copy(out=sb_thr[:, 2 * wi:2 * wi + 1], in_=pbank[7][:, 0:1])
            v_.tensor_copy(out=sb_thr[:, 2 * wi + 1:2 * wi + 2],
                           in_=pbank[7][:, 1:2], inc=inc1(s_v))

        mark('x1x2')
        v_.wait(s_lte[0], lte_done[0])
        v_.wait(s_lte[1], lte_done[1])
        x1_marks = []
        x1_vmarks = []
        ep1_marks = []
        hid_done = [0, 0]
        NU1 = 44

        def x1_fill(u):
            row = u * 128
            if u == 0:
                q_.wait(s_mm)
                q_.wait(s_v)
                g_.wait(s_mm)
                g_.wait(s_v)
            else:
                q_.wait(s_v, x1_vmarks[u - 1])
                g_.wait(s_v, x1_vmarks[u - 1])
            q_.dma_start(out=xw1_stg[:, :], in_=w1p[row:row + 128, :],
                         inc=dinc(s_x1w[0]))
            g_.dma_start(out=xw2_stg[:, :], in_=w2p[row:row + 128, :],
                         inc=dinc(s_x1w[1]))

        def x1_quant(u):
            buf = u % 2
            if u >= 2:
                v_.wait(s_mm, x1_marks[u - 2])
            v_.wait(s_x1w[0])
            v_.drain()
            v_.tensor_single_scalar(xw_tmp[:, :], xw1_stg[:, :],
                                    sb_thr[:, 1:2], OP.is_lt)
            v_.drain()
            v_.scalar_tensor_tensor(q1s[buf][:, :], xw1_stg[:, :],
                                    sb_thr[:, 0:1], xw_tmp[:, :],
                                    OP.is_gt, OP.subtract)
            v_.wait(s_x1w[1])
            v_.drain()
            v_.tensor_single_scalar(xw_tmp2[:, :], xw2_stg[:, :],
                                    sb_thr[:, 3:4], OP.is_lt)
            v_.drain()
            v_.scalar_tensor_tensor(q2s[buf][:, :], xw2_stg[:, :],
                                    sb_thr[:, 2:3], xw_tmp2[:, :],
                                    OP.is_gt, OP.subtract, inc=inc1(s_v))
            x1_vmarks.append(s_v.n)

        x1_fill(0)
        x1_quant(0)
        for u in range(NU1):
            mg, kh = u // 2, u % 2
            buf = u % 2
            if u + 1 < NU1:
                x1_fill(u + 1)
                x1_quant(u + 1)
            if kh == 0 and mg >= 1:
                t_.wait(s_v, ep1_marks[mg - 1])
            else:
                t_.wait(s_v, x1_vmarks[u])
            for kc8 in range(8):
                kc = kh * 8 + kc8
                st = (kc == 0); sp = (kc == NCH - 1)
                for mc in range(2):
                    w1t = q1s[buf][:, kc8 * 256 + mc * 128:
                                   kc8 * 256 + (mc + 1) * 128]
                    w2t = q2s[buf][:, kc8 * 256 + mc * 128:
                                   kc8 * 256 + (mc + 1) * 128]
                    for tb in range(2):
                        xh = xn_piece(True, kc, ts(tb, 512))
                        xl = xn_piece(False, kc, ts(tb, 512))
                        p1 = pbank[mc * 2 + tb]
                        p2 = pbank[4 + mc * 2 + tb]
                        t_.matmul(p1[:, :], w1t, xh, start=st, stop=False)
                        t_.matmul(p1[:, :], w1t, xl, start=False, stop=sp)
                        t_.matmul(p2[:, :], w2t, xh, start=st, stop=False)
                        t_.matmul(p2[:, :], w2t, xl, start=False, stop=sp,
                                  inc=inc1(s_mm)
                                  if (kc8 == 7 and mc == 1 and tb == 1) else None)
            x1_marks.append(s_mm.n)
            if kh == 1:
                s_.wait(s_mm, x1_marks[u])
                for mc in range(2):
                    mabs = mg * 2 + mc
                    hb = mabs % 2
                    s_.wait(s_v)
                    for tb in range(2):
                        s_.activation(out=hd_s[:, ts(tb, 512)],
                                      in_=pbank[mc * 2 + tb][:, :],
                                      func=AF.Sigmoid,
                                      inc=inc1(s_s) if tb == 1 else None)
                    v_.wait(s_s)
                    for tb in range(2):
                        v_.tensor_tensor(hd_f[:, ts(tb, 512)], hd_s[:, ts(tb, 512)],
                                         pbank[4 + mc * 2 + tb][:, :], OP.mult)
                    if hid_done[hb]:
                        v_.wait(s_hid[hb], hid_done[hb])
                    v_.drain()
                    v_.tensor_copy(out=hd_h[hb][:, :], in_=hd_f[:, :])
                    v_.drain()
                    v_.tensor_tensor(hd_l[hb][:, :], hd_f[:, :], hd_h[hb][:, :],
                                     OP.subtract, inc=inc1(s_v))
                    q_.wait(s_v)
                    q_.dma_start(out=hid_hi[ts(mabs, 128), :], in_=hd_h[hb][:, :],
                                 inc=dinc(s_hid[hb]))
                    q_.dma_start(out=hid_lo[ts(mabs, 128), :], in_=hd_l[hb][:, :],
                                 inc=dinc(s_hid[hb]))
                    hid_done[hb] = s_hid[hb].n
                ep1_marks.append(s_v.n)

        mark('kvlat')
        g_.wait(s_cc, ag1_done)
        g_.wait(s_v)
        g_.wait(s_mm)
        g_.wait(s_hid[0], hid_done[0])
        g_.wait(s_hid[1], hid_done[1])
        q_.wait(s_cc, ag1_done)
        q_.wait(s_v)
        q_.wait(s_mm)
        q_.wait(s_hid[0], hid_done[0])
        q_.wait(s_hid[1], hid_done[1])
        for kc in range(4):
            for sl in range(2):
                row0 = sl * 2048 + 1536 + kc * 128
                q_.dma_start(out=kvstg[:, ts(sl, 1024)],
                             in_=lat_out[row0:row0 + 128, :], inc=dinc(s_kvla))
                g_.dma_start(out=kvl_hi[:, kc * 2048 + sl * 1024:
                                        kc * 2048 + (sl + 1) * 1024],
                             in_=lat_out[row0:row0 + 128, :], inc=dinc(s_kvlb))
            v_.wait(s_kvla)
            v_.wait(s_kvlb)
            v_.tensor_tensor(kvl_lo[:, ts(kc, 2048)], kvstg[:, :],
                             kvl_hi[:, ts(kc, 2048)], OP.subtract, inc=inc1(s_v))
            if kc < 3:
                q_.wait(s_v)
                g_.wait(s_v)

        def split_psum(ps_ap, hi_ap, lo_ap):
            s_.copy(out=hi_ap, in_=ps_ap, inc=inc1(s_s))
            v_.wait(s_s)
            v_.scalar_tensor_tensor(lo_ap, ps_ap, 0.0, hi_ap,
                                    OP.add, OP.subtract, inc=inc1(s_v))

        def rope_split(ps_ap, col, hi_ap, lo_ap):
            cs = sb_cos[:, col:col + 512]
            sn = sb_sin[:, col:col + 512]
            v_.drain()
            v_.tensor_tensor(rsc1[:, :], ps_ap, cs, OP.mult)
            v_.stream_shuffle(out=rsc2[:, :], in_=ps_ap,
                              mask=[i ^ 1 for i in range(32)])
            v_.drain()
            v_.tensor_tensor(rsc3[:, :], rsc2[:, :], sn, OP.mult)
            v_.drain()
            v_.tensor_tensor(rsc2[:, :], rsc1[:, :], rsc3[:, :], OP.add,
                             inc=inc1(s_v))
            s_.wait(s_v)
            s_.copy(out=hi_ap, in_=rsc2[:, :], inc=inc1(s_s))
            v_.wait(s_s)
            v_.tensor_tensor(lo_ap, rsc2[:, :], hi_ap, OP.subtract,
                             inc=inc1(s_v))

        mark('attn')
        qk_copy_marks = []
        qk_exp_marks = []
        av_done = 0
        for lh in range(8):
            q_.wait(s_mm)
            g_.wait(s_mm)
            if lh == 0:
                q_.wait(s_v)
                g_.wait(s_v)
            for kh in range(2):
                row = (lh * 2 + kh) * 128
                if kh == 1:
                    q_.wait(s_v)
                q_.dma_start(out=qw_stg[:, :], in_=wqwp[row:row + 128, :],
                             inc=dinc(s_qwsa[kh]))
                g_.dma_start(out=qw_hi[kh][:, :], in_=wqwp[row:row + 128, :],
                             inc=dinc(s_qwsb[kh]))
                v_.wait(s_qwsa[kh])
                v_.wait(s_qwsb[kh])
                v_.drain()
                v_.tensor_tensor(qw_lo[kh][:, :], qw_stg[:, :],
                                 qw_hi[kh][:, :], OP.subtract, inc=inc1(s_v))
            qe_marks = []
            for th in range(2):
                for kc in range(12):
                    i = th * 12 + kc
                    buf = kc % 2
                    if i >= 2:
                        q_.wait(s_mm, qe_marks[i - 2])
                        g_.wait(s_mm, qe_marks[i - 2])
                    else:
                        q_.wait(s_mm)
                        g_.wait(s_mm)
                    row0 = th * 2048 + kc * 128
                    q_.dma_start(out=ql_stg[buf][:, :],
                                 in_=lat_out[row0:row0 + 128, :],
                                 inc=dinc(s_qla2[buf]))
                    g_.dma_start(out=ql_hi[buf][:, :],
                                 in_=lat_out[row0:row0 + 128, :],
                                 inc=dinc(s_qlb2[buf]))
                    v_.wait(s_qla2[buf])
                    v_.wait(s_qlb2[buf])
                    v_.drain()
                    v_.tensor_tensor(ql_lo[buf][:, :], ql_stg[buf][:, :],
                                     ql_hi[buf][:, :], OP.subtract, inc=inc1(s_v))
                    t_.wait(s_v)
                    st = (kc == 0); sp = (kc == 11)
                    kh = kc // 6
                    off = (kc % 6) * 192
                    wnh = qw_hi[kh][:, off:off + 128]
                    wnl = qw_lo[kh][:, off:off + 128]
                    wrh = qw_hi[kh][:, off + 128:off + 192]
                    wrl = qw_lo[kh][:, off + 128:off + 192]
                    for tb in range(2):
                        qlh = ql_hi[buf][:, ts(tb, 512)]
                        qll = ql_lo[buf][:, ts(tb, 512)]
                        pn = pbank[tb]
                        pr = pbank[2 + tb][0:64, :]
                        t_.matmul(pn[:, :], wnh, qlh, start=st, stop=False)
                        t_.matmul(pn[:, :], wnh, qll, start=False, stop=False)
                        t_.matmul(pr, wrh, qlh, start=st, stop=False)
                        t_.matmul(pr, wrh, qll, start=False, stop=False)
                    for tb in range(2):
                        qlh = ql_hi[buf][:, ts(tb, 512)]
                        pn = pbank[tb]
                        pr = pbank[2 + tb][0:64, :]
                        t_.matmul(pn[:, :], wnl, qlh, start=False, stop=sp)
                        t_.matmul(pr, wrl, qlh, start=False, stop=sp,
                                  inc=inc1(s_mm) if tb == 1 else None)
                    qe_marks.append(s_mm.n)
                s_.wait(s_mm)
                v_.wait(s_mm)
                if av_done:
                    v_.wait(s_av, av_done)
                for tb in range(2):
                    col = th * 1024 + tb * 512
                    split_psum(pbank[tb][:, :], qn_hi[:, col:col + 512],
                               qn_lo[:, col:col + 512])
                    rope_split(pbank[2 + tb][0:64, :], col,
                               qr_hi[:, col:col + 512], qr_lo[:, col:col + 512])

            q_.wait(s_mm)
            g_.wait(s_mm)
            q_.dma_start(out=kw_stg[:, :], in_=wkwp[ts(lh, 128), :],
                         inc=dinc(s_kw))
            g_.dma_start(out=kw_hi[:, :], in_=wkwp[ts(lh, 128), :],
                         inc=dinc(s_kwb))
            v_.wait(s_kw)
            v_.wait(s_kwb)
            v_.tensor_tensor(kw_lo[:, :], kw_stg[:, :], kw_hi[:, :], OP.subtract,
                             inc=inc1(s_v))
            t_.wait(s_v)
            for kc in range(4):
                st = (kc == 0); sp = (kc == 3)
                wnh = kw_hi[:, kc * 128:(kc + 1) * 128]
                wnl = kw_lo[:, kc * 128:(kc + 1) * 128]
                wrh = kw_hi[:, 512 + kc * 64:512 + (kc + 1) * 64]
                wrl = kw_lo[:, 512 + kc * 64:512 + (kc + 1) * 64]
                for tb in range(4):
                    kvh = kvl_hi[:, kc * 2048 + tb * 512: kc * 2048 + (tb + 1) * 512]
                    kvl_ = kvl_lo[:, kc * 2048 + tb * 512: kc * 2048 + (tb + 1) * 512]
                    t_.matmul(pbank[tb][:, :], wnh, kvh, start=st, stop=False)
                    t_.matmul(pbank[tb][:, :], wnh, kvl_, start=False, stop=False)
                    t_.matmul(pbank[4 + tb][0:64, :], wrh, kvh, start=st, stop=False)
                    t_.matmul(pbank[4 + tb][0:64, :], wrh, kvl_, start=False, stop=False)
                for tb in range(4):
                    kvh = kvl_hi[:, kc * 2048 + tb * 512: kc * 2048 + (tb + 1) * 512]
                    t_.matmul(pbank[tb][:, :], wnl, kvh, start=False, stop=sp)
                    t_.matmul(pbank[4 + tb][0:64, :], wrl, kvh, start=False, stop=sp,
                              inc=inc1(s_mm) if (sp and tb == 3) else None)
            s_.wait(s_mm)
            v_.wait(s_mm)
            if av_done:
                v_.wait(s_av, av_done)
            for tb in range(2):
                col = tb * 512
                split_psum(pbank[tb][:, :], kn_hi[:, col:col + 512],
                           kn_lo[:, col:col + 512])
            q_.wait(s_mm)
            g_.wait(s_mm)
            q_.dma_start(out=vw_stg[:, :], in_=wvwp[ts(lh, 128), :],
                         inc=dinc(s_vw))
            g_.dma_start(out=vw_hi[:, :], in_=wvwp[ts(lh, 128), :],
                         inc=dinc(s_vwb))
            v_.wait(s_vw)
            v_.wait(s_vwb)
            v_.drain()
            v_.tensor_tensor(vw_lo[:, :], vw_stg[:, :], vw_hi[:, :], OP.subtract,
                             inc=inc1(s_v))
            vwlo_mark = s_v.n
            for tb in range(2, 4):
                col = tb * 512
                split_psum(pbank[tb][:, :], kn_hi[:, col:col + 512],
                           kn_lo[:, col:col + 512])
            for tb in range(4):
                col = tb * 512
                rope_split(pbank[4 + tb][0:64, :], col,
                           kr_hi[:, col:col + 512], kr_lo[:, col:col + 512])
            t_.wait(s_v, vwlo_mark)
            vev_marks = []
            for tc in range(16):
                pb = pbank[tc % 2]
                if tc >= 2:
                    t_.wait(s_v, vev_marks[tc - 2])
                for kc in range(4):
                    lvh = kvl_hi[:, kc * 2048 + tc * 128: kc * 2048 + tc * 128 + 128]
                    lvl = kvl_lo[:, kc * 2048 + tc * 128: kc * 2048 + tc * 128 + 128]
                    wh = vw_hi[:, kc * 192:(kc + 1) * 192]
                    wl = vw_lo[:, kc * 192:(kc + 1) * 192]
                    t_.matmul(pb[:, 0:192], lvh, wh, start=(kc == 0), stop=False)
                    t_.matmul(pb[:, 0:192], lvh, wl, start=False, stop=False)
                    t_.matmul(pb[:, 0:192], lvl, wh, start=False, stop=(kc == 3),
                              inc=inc1(s_mm) if kc == 3 else None)
                s_.wait(s_mm)
                v_.wait(s_mm)
                split_psum(pb[:, 0:192], av_hi[:, tc * 192:(tc + 1) * 192],
                           av_lo[:, tc * 192:(tc + 1) * 192])
                vev_marks.append(s_v.n)

            for qp in range(8):
                span = (qp + 1) * 256
                nkb = span // 256
                t_.wait(s_v)
                for q2 in range(2):
                    qt = 2 * qp + q2
                    qsl = slice(qt * 128, (qt + 1) * 128)
                    if q2 == 1:
                        t_.wait(s_v, qk_copy_marks[-1])
                    passes = [
                        (qn_hi[:, qsl], kn_hi, True, False),
                        (qn_hi[:, qsl], kn_lo, False, False),
                        (qn_lo[:, qsl], kn_hi, False, False),
                        (qr_hi[:, qsl], kr_hi, False, False),
                        (qr_hi[:, qsl], kr_lo, False, False),
                        (qr_lo[:, qsl], kr_hi, False, True),
                    ]
                    for (lt, rt, isfirst, islast) in passes:
                        for kb in range(nkb):
                            stf = isfirst and (kb % 2 == 0)
                            spf = islast and (kb % 2 == 1 or kb == nkb - 1)
                            t_.matmul(psA[:, kb * 256:(kb + 1) * 256],
                                      lt, rt[:, kb * 256:(kb + 1) * 256],
                                      start=stf, stop=spf,
                                      inc=inc1(s_mm)
                                      if (islast and kb == nkb - 1) else None)
                    v_.wait(s_mm)
                    if qk_exp_marks:
                        v_.wait(s_s, qk_exp_marks[-1])
                    v_.drain()
                    v_.tensor_copy(out=p_f[:, 0:span], in_=psA[:, 0:span],
                                   inc=inc1(s_v))
                    qk_copy_marks.append(s_v.n)
                    v_.drain()
                    mask = sb_maske if qt % 2 == 0 else sb_masko
                    v_.tensor_tensor(p_f[:, span - 256:span],
                                     p_f[:, span - 256:span], mask[:, :], OP.add)
                    v_.drain()
                    v_.reduce_max(sb_sm[:, 0:1], p_f[:, 0:span],
                                  axis=mybir.AxisListType.X, negate=True)
                    v_.drain()
                    v_.tensor_scalar_mul(sb_sm[:, 1:2], sb_sm[:, 0:1], CATT,
                                         inc=inc1(s_v))
                    s_.wait(s_v)
                    s_.activation(out=p_f[:, 0:span], in_=p_f[:, 0:span],
                                  func=AF.Exp, scale=CATT, bias=sb_sm[:, 1:2],
                                  accum_out=sb_sm[:, 2:3], inc=inc1(s_s))
                    qk_exp_marks.append(s_s.n)
                    v_.wait(s_s)
                    v_.reciprocal(sb_sm[:, 3:4], sb_sm[:, 2:3])
                    v_.drain()
                    v_.tensor_scalar_mul(ph_p[:, q2 * span:q2 * span + span],
                                         p_f[:, 0:span], sb_sm[:, 3:4])
                    v_.drain()
                    v_.scalar_tensor_tensor(pl_p[:, q2 * span:q2 * span + span],
                                            p_f[:, 0:span], sb_sm[:, 3:4],
                                            ph_p[:, q2 * span:q2 * span + span],
                                            OP.mult, OP.subtract, inc=inc1(s_v))
                t_.wait(s_v)
                nkc = span // 128
                tr_marks = []
                ev_marks = []
                pv_marks = []

                def pt_sl(set_, pi, q2):
                    base = pi * 256 + q2 * 128
                    t_ = pt_ps if set_ == 0 else pt_ps2
                    return t_[:, base:base + 128]

                def emit_transp(kc):
                    set_ = kc % 2
                    if kc >= 2:
                        t_.wait(s_v, ev_marks[kc - 2])
                    for (pi, ptile) in enumerate((ph_p, pl_p)):
                        for q2 in range(2):
                            t_.transpose(pt_sl(set_, pi, q2),
                                         ptile[:, q2 * span + kc * 128:
                                               q2 * span + (kc + 1) * 128],
                                         sb_ident[:, :],
                                         inc=inc1(s_mm)
                                         if (pi == 1 and q2 == 1) else None)
                    tr_marks.append(s_mm.n)

                def emit_evac(kc):
                    set_ = kc % 2
                    v_.wait(s_mm, tr_marks[kc])
                    if kc >= 2:
                        v_.wait(s_mm, pv_marks[kc - 2])
                    src_t = pt_ps if set_ == 0 else pt_ps2
                    v_.tensor_copy(out=ptr_hi[:, ts(set_, 256)],
                                   in_=src_t[:, 0:256])
                    v_.tensor_copy(out=ptr_lo[:, ts(set_, 256)],
                                   in_=src_t[:, 256:512],
                                   inc=inc1(s_v))
                    ev_marks.append(s_v.n)

                emit_transp(0)
                emit_evac(0)
                for kc in range(nkc):
                    set_ = kc % 2
                    if kc + 1 < nkc:
                        emit_transp(kc + 1)
                    t_.wait(s_v, ev_marks[kc])
                    st = (kc == 0); sp = (kc == nkc - 1)
                    vh0 = av_hi[:, kc * 192:kc * 192 + 128]
                    vl0 = av_lo[:, kc * 192:kc * 192 + 128]
                    vh1 = av_hi[:, kc * 192 + 128:kc * 192 + 192]
                    vl1 = av_lo[:, kc * 192 + 128:kc * 192 + 192]
                    pa = pbank[4][:, 0:256]
                    pb_ = pbank[5][0:64, 0:256]
                    ph_s = ptr_hi[:, ts(set_, 256)]
                    pl_s = ptr_lo[:, ts(set_, 256)]
                    t_.matmul(pa, vh0, ph_s, start=st, stop=False)
                    t_.matmul(pa, vh0, pl_s, start=False, stop=False)
                    t_.matmul(pa, vl0, ph_s, start=False, stop=sp)
                    t_.matmul(pb_, vh1, ph_s, start=st, stop=False)
                    t_.matmul(pb_, vh1, pl_s, start=False, stop=False)
                    t_.matmul(pb_, vl1, ph_s, start=False, stop=sp,
                              inc=inc1(s_mm))
                    pv_marks.append(s_mm.n)
                    if kc + 1 < nkc:
                        emit_evac(kc + 1)
                v_.wait(s_mm)
                if av_done:
                    v_.wait(s_av, av_done)
                v_.tensor_copy(out=pv_st[:, 0:256], in_=pbank[4][:, 0:256])
                v_.tensor_copy(out=pv_st[0:64, 256:512], in_=pbank[5][0:64, 0:256],
                               inc=inc1(s_v))
                q_.wait(s_v)
                r0 = lh * 192
                q_.dma_start(out=aout1[r0:r0 + 128, ts(qp, 256)],
                             in_=pv_st[:, 0:256], inc=dinc(s_av))
                q_.dma_start(out=aout1[r0 + 128:r0 + 192, ts(qp, 256)],
                             in_=pv_st[0:64, 256:512], inc=dinc(s_av))
                av_done = s_av.n

        g_.wait(s_av, av_done)
        g_.collective_compute(
            "AllGather", OP.bypass, replica_groups=RG,
            ins=[aout1[:, :]], outs=[aout1_out[:, :]], inc=inc1(s_cc))
        ag2_done = s_cc.n

        mark('down1')
        dn1_marks = []
        qv_marks = []
        u = 0
        for tb in range(2):
            for mcg in range(2):
                t_.wait(s_v)
                for kcq in range(11):
                    buf = u % 2
                    for sf, (eng, sem) in enumerate(((g_, s_w3s[0]),
                                                     (q_, s_w3s[1]))):
                        if u >= 1:
                            eng.wait(s_v, qv_marks[u - 1])
                        else:
                            eng.wait(s_mm)
                            eng.wait(s_v)
                            eng.wait(s_hid[0], hid_done[0])
                            eng.wait(s_hid[1], hid_done[1])
                        r0 = kcq * 512 + sf * 256
                        eng.dma_start(
                            out=sb3(dn3_stg[sf][:, :], 1024),
                            in_=w3p[r0:r0 + 256, ts(mcg, 1024)]
                                .rearrange("(c p) m -> p c m", p=128),
                            inc=dinc(sem))
                    if u >= 2:
                        g_.wait(s_mm, dn1_marks[u - 2])
                        q_.wait(s_mm, dn1_marks[u - 2])
                    for kc4 in range(4):
                        kc = kcq * 4 + kc4
                        g_.dma_start(out=dn_fh[buf][:, ts(kc4, 512)],
                                     in_=hid_hi[ts(kc, 128), ts(tb, 512)],
                                     inc=dinc(s_dn[buf]))
                        q_.dma_start(out=dn_fl[buf][:, ts(kc4, 512)],
                                     in_=hid_lo[ts(kc, 128), ts(tb, 512)],
                                     inc=dinc(s_dn1[buf]))
                    if u >= 2:
                        v_.wait(s_mm, dn1_marks[u - 2])
                    elif u == 0:
                        v_.wait(s_mm)
                    for sf in range(2):
                        v_.wait(s_w3s[sf])
                        v_.drain()
                        v_.tensor_single_scalar(w3_tmp2[:, :], dn3_stg[sf][:, :],
                                                sb_thr[:, 5:6], OP.is_lt)
                        v_.drain()
                        v_.scalar_tensor_tensor(
                            dn_w[buf][:, sf * 2048:(sf + 1) * 2048],
                            dn3_stg[sf][:, :], sb_thr[:, 4:5], w3_tmp2[:, :],
                            OP.is_gt, OP.subtract,
                            inc=inc1(s_v) if sf == 1 else None)
                    qv_marks.append(s_v.n)
                    t_.wait(s_v, qv_marks[u])
                    t_.wait(s_dn[buf])
                    t_.wait(s_dn1[buf])
                    for kc4 in range(4):
                        st = (kcq == 0 and kc4 == 0); sp = (kcq == 10 and kc4 == 3)
                        for mc in range(8):
                            wt = dn_w[buf][:, kc4 * 1024 + mc * 128:
                                           kc4 * 1024 + (mc + 1) * 128]
                            po = pbank[mc]
                            t_.matmul(po[:, :], wt, dn_fh[buf][:, ts(kc4, 512)],
                                      start=st, stop=False)
                            t_.matmul(po[:, :], wt, dn_fl[buf][:, ts(kc4, 512)],
                                      start=False, stop=sp,
                                      inc=inc1(s_mm)
                                      if (kc4 == 3 and mc == 7) else None)
                    dn1_marks.append(s_mm.n)
                    u += 1
                v_.wait(s_mm)
                for mc in range(8):
                    mabs = mcg * 8 + mc
                    xd = x_res[:, mabs * 1024 + tb * 512: mabs * 1024 + (tb + 1) * 512]
                    v_.tensor_tensor(xd, pbank[mc][:, :], xd, OP.add,
                                     inc=inc1(s_v) if mc == 7 else None)

        mark('wo1')
        g_.wait(s_cc, ag2_done)
        q_.wait(s_cc, ag2_done)
        wo1_marks = []
        w1r_done = [0, 0]
        u = 0
        for mcg in range(4):
            t_.wait(s_v)
            for kc in range(24):
                buf = u % 2
                for eng in (q_, g_):
                    if u >= 2:
                        eng.wait(s_mm, wo1_marks[u - 2])
                    else:
                        eng.wait(s_mm)
                        eng.wait(s_v)
                q_.dma_start(out=wo_stg[buf][:, :],
                             in_=wo1[ts(kc, 128), ts(mcg, 512)], inc=dinc(s_lt[buf]))
                g_.dma_start(out=wo_hi[buf][:, :],
                             in_=wo1[ts(kc, 128), ts(mcg, 512)], inc=dinc(s_ltb[buf]))

                def mk_dyn(dst, krow, sem):
                    def op(eng):
                        inst = eng.dma_start(
                            out=dst,
                            in_=aout1_out[krow:krow + 128,
                                          bass.ds(tok_cell["v"], 1024)])
                        inst.then_inc(sem.h, 16)
                    return op
                s_w1r[buf].n += 16
                q_.ops.append(mk_dyn(w1r_stg[buf][:, :], kc * 128, s_w1r[buf]))
                v_.wait(s_lt[buf])
                v_.wait(s_ltb[buf])
                v_.tensor_tensor(wo_lo[buf][:, :], wo_stg[buf][:, :],
                                 wo_hi[buf][:, :], OP.subtract)
                v_.wait(s_w1r[buf])
                v_.tensor_copy(out=w1r_hi[buf][:, :], in_=w1r_stg[buf][:, :])
                v_.drain()
                v_.tensor_tensor(w1r_lo[buf][:, :], w1r_stg[buf][:, :],
                                 w1r_hi[buf][:, :], OP.subtract, inc=inc1(s_v))
                t_.wait(s_v)
                st = (kc == 0); sp = (kc == 23)
                for mc in range(4):
                    wht = wo_hi[buf][:, ts(mc, 128)]
                    wlt = wo_lo[buf][:, ts(mc, 128)]
                    for tb in range(2):
                        rh = w1r_hi[buf][:, ts(tb, 512)]
                        rl = w1r_lo[buf][:, ts(tb, 512)]
                        po = pbank[mc * 2 + tb]
                        t_.matmul(po[:, :], wht, rh, start=st, stop=False)
                        t_.matmul(po[:, :], wht, rl, start=False, stop=False)
                    for tb in range(2):
                        rh = w1r_hi[buf][:, ts(tb, 512)]
                        po = pbank[mc * 2 + tb]
                        t_.matmul(po[:, :], wlt, rh, start=False, stop=sp,
                                  inc=inc1(s_mm) if (mc == 3 and tb == 1) else None)
                wo1_marks.append(s_mm.n)
                u += 1
            v_.wait(s_mm)
            for mc in range(4):
                mabs = mcg * 4 + mc
                for tb in range(2):
                    xd = x_res[:, mabs * 1024 + tb * 512: mabs * 1024 + (tb + 1) * 512]
                    v_.tensor_tensor(xd, pbank[mc * 2 + tb][:, :], xd, OP.add,
                                     inc=inc1(s_v) if (mc == 3 and tb == 1) else None)

    for name in dbg:
        if name == "xn":
            dh = dout("dbg_xnh", [128, 16384], BF16)
            dl = dout("dbg_xnl", [128, 16384], BF16)
            g_.wait(s_v)
            g_.dma_start(out=dh[:, :], in_=xn_hi[:, :], inc=dinc(s_dbg))
            g_.dma_start(out=dl[:, :], in_=xn_lo[:, :], inc=dinc(s_dbg))
        if name == "kv":
            d = dout("dbg_kv", [64, 2080])
            g_.wait(s_v)
            g_.dma_start(out=d[:, :], in_=kv_sb[:, :], inc=dinc(s_dbg))
        if name == "aout0":
            dh = dout("dbg_aout0h", [2048, 1024], BF16)
            dl = dout("dbg_aout0l", [2048, 1024], BF16)
            g_.wait(s_ao, aout_done)
            g_.dma_start(out=dh[:, :], in_=aout0_hi[:, :], inc=dinc(s_dbg))
            g_.dma_start(out=dl[:, :], in_=aout0_lo[:, :], inc=dinc(s_dbg))
        if name == "fuse":
            dh = dout("dbg_fuseh", [8192, 1024], BF16)
            g_.wait(s_fu[0], fuse_done[0])
            g_.wait(s_fu[1], fuse_done[1])
            g_.dma_start(out=dh[:, :], in_=fuse_hi[:, :], inc=dinc(s_dbg))

    q_.wait(s_v)
    q_.dma_start(out=chunked(y), in_=sb3(x_res[:, :], 1024), inc=dinc(s_st))
    q_.wait(s_st)
    g_.wait(s_dbg)

    p.finalize()
    return nc



def make_tables():
    ident = np.eye(128, dtype=np.float32)
    ones = np.ones((1, 128), dtype=np.float32)
    DR = 64
    inv = 1.0 / (10000.0 ** (np.arange(0, DR, 2, dtype=np.float32) / DR))
    t = np.arange(2048, dtype=np.float32)
    fr = np.outer(t, inv)
    cos = np.cos(fr).astype(np.float32)
    sin = np.sin(fr).astype(np.float32)
    cosd = np.zeros((64, 2048), np.float32)
    sind = np.zeros((64, 2048), np.float32)
    cosd[0::2] = cos.T; cosd[1::2] = cos.T
    sind[0::2] = -sin.T; sind[1::2] = sin.T
    r = np.arange(128)[:, None]; c = np.arange(256)[None, :]
    mask_e = np.where(c <= r, 0.0, NEG_BIG).astype(np.float32)
    mask_o = np.where((c - 128) <= r, 0.0, NEG_BIG).astype(np.float32)
    return ident, ones, cosd, sind, mask_e, mask_o


def make_in_maps(inp):
    import ml_dtypes
    BF = ml_dtypes.bfloat16
    ident, ones, cosd, sind, mask_e, mask_o = make_tables()
    x = np.asarray(inp["L0_attn_hidden_states"])
    in_maps = []
    for c in range(8):
        b, hf = c // 2, c % 2
        wq = np.asarray(inp["L0_attn_Wq"]); wk = np.asarray(inp["L0_attn_Wk"])
        wv = np.asarray(inp["L0_attn_Wv"])
        wg8 = np.asarray(inp["L0_ffn_w_gate"]); wu8 = np.asarray(inp["L0_ffn_w_up"])
        wd8 = np.asarray(inp["L0_ffn_w_down"])
        w1 = np.asarray(inp["L1_ffn_w1"]); w2 = np.asarray(inp["L1_ffn_w2"])
        w3 = np.asarray(inp["L1_ffn_w3"])
        wqn = np.asarray(inp["L1_attn_wq_nope"]).reshape(1536, 16, 128)[:, hf * 8:(hf + 1) * 8].reshape(1536, 1024)
        wqr = np.asarray(inp["L1_attn_wq_rope"]).reshape(1536, 16, 64)[:, hf * 8:(hf + 1) * 8].reshape(1536, 512)
        wkn = np.asarray(inp["L1_attn_wk_nope"]).reshape(512, 16, 128)[:, hf * 8:(hf + 1) * 8].reshape(512, 1024)
        wkr = np.asarray(inp["L1_attn_wk_rope"]).reshape(512, 16, 64)[:, hf * 8:(hf + 1) * 8].reshape(512, 512)
        wvu = np.asarray(inp["L1_attn_wv_up"]).reshape(512, 16, 192)[:, hf * 8:(hf + 1) * 8].reshape(512, 1536)
        wqp = wq.reshape(16, 128, 32, 64).transpose(2, 1, 0, 3).reshape(4096, 1024)
        wkp = wk.reshape(16, 128, 8, 256).transpose(2, 1, 0, 3).reshape(1024, 4096)
        wvp = wv.reshape(16, 128, 8, 256).transpose(2, 1, 0, 3).reshape(1024, 4096)
        wg8p = wg8.reshape(16, 128, 32, 256).transpose(2, 1, 0, 3).reshape(4096, 4096)
        wu8p = wu8.reshape(16, 128, 32, 256).transpose(2, 1, 0, 3).reshape(4096, 4096)
        wdp = wd8.reshape(16, 4, 128, 2, 1024).transpose(3, 0, 2, 1, 4).reshape(4096, 4096)
        w1p = w1.reshape(22, 256, 2, 8, 128).transpose(0, 2, 4, 3, 1).reshape(5632, 2048)
        w2p = w2.reshape(22, 256, 2, 8, 128).transpose(0, 2, 4, 3, 1).reshape(5632, 2048)
        w3p = w3.reshape(4, 512, 44, 128).transpose(2, 3, 0, 1).reshape(5632, 2048)
        qn4 = wqn.reshape(12, 128, 8, 128).transpose(2, 0, 1, 3)
        qr4 = wqr.reshape(12, 128, 8, 64).transpose(2, 0, 1, 3)
        wqwp = np.concatenate([qn4, qr4], axis=3)
        wqwp = wqwp.reshape(8, 2, 6, 128, 192).transpose(0, 1, 3, 2, 4).reshape(2048, 1152)
        kn4 = wkn.reshape(4, 128, 8, 128).transpose(2, 1, 0, 3).reshape(8, 128, 512)
        kr4 = wkr.reshape(4, 128, 8, 64).transpose(2, 1, 0, 3).reshape(8, 128, 256)
        wkwp = np.concatenate([kn4, kr4], axis=2).reshape(1024, 768)
        wvwp = wvu.reshape(4, 128, 8, 192).transpose(2, 1, 0, 3).reshape(1024, 768)
        m = {
            "x_in": x[b, hf * 1024:(hf + 1) * 1024, :].T,
            "wqp": wqp, "wkp": wkp, "wvp": wvp, "wo0": inp["L0_attn_Wo"],
            "bo_t": np.asarray(inp["L0_attn_bo"]).reshape(16, 128).T,
            "wg8p": wg8p, "wu8p": wu8p, "wdp": wdp,
            "sg_t": np.asarray(inp["L0_ffn_s_gate"]).reshape(64, 128).T,
            "su_t": np.asarray(inp["L0_ffn_s_up"]).reshape(64, 128).T,
            "sd_t": np.asarray(inp["L0_ffn_s_down"]).reshape(16, 128).T,
            "wqd": inp["L1_attn_wq_down"], "wkvd": inp["L1_attn_wkv_down"],
            "wqwp": wqwp, "wkwp": wkwp, "wvwp": wvwp,
            "wo1": inp["L1_attn_wo"],
            "w1p": w1p, "w2p": w2p, "w3p": w3p,
            "tok_off": np.array([[hf * 1024]], np.int32),
            "ident_bf": ident.astype(BF), "ones_f": ones,
            "cosd": cosd, "sind": sind,
            "mask_e": mask_e.astype(BF), "mask_o": mask_o.astype(BF),
        }
        in_maps.append({k: np.ascontiguousarray(np.asarray(v)) for k, v in m.items()})
    return in_maps


def kernel(**inp):
    nc = build_program()
    in_maps = make_in_maps(inp)
    res = run_bass_kernel_spmd(nc, in_maps, core_ids=list(range(8)))
    out = np.empty((4, 2048, 2048), np.float32)
    for c in range(8):
        b, hf = c // 2, c % 2
        out[b, hf * 1024:(hf + 1) * 1024, :] = np.asarray(res.results[c]["y"]).T
    return out


# version 13
# speedup vs baseline: 1.0461x; 1.0461x over previous
import numpy as np
import concourse.bass as bass
import concourse.mybir as mybir
from concourse.bass_utils import run_bass_kernel_spmd

F32 = mybir.dt.float32
BF16 = mybir.dt.bfloat16
I8 = mybir.dt.int8
AF = mybir.ActivationFunctionType
OP = mybir.AluOpType

NEG_BIG = -1e30
NCH = 16

ENGINES = ("sync", "tensor", "vector", "scalar", "gpsimd")


class Sem:
    def __init__(self, handle):
        self.h = handle
        self.n = 0


class Eng:
    def __init__(self, name):
        self.name = name
        self.ops = []
        self.kinds = []

    def _push(self, fn_name, a, kw, incs):
        self.kinds.append(fn_name)
        def op(eng):
            inst = getattr(eng, fn_name)(*a, **kw)
            for (sem, k) in incs:
                inst.then_inc(sem.h, k)
        self.ops.append(op)

    def __getattr__(self, fn_name):
        if fn_name.startswith("_"):
            raise AttributeError(fn_name)

        def call(*a, inc=None, **kw):
            incs = []
            if inc is not None:
                for (sem, k) in inc:
                    sem.n += k
                    incs.append((sem, k))
            self._push(fn_name, a, kw, incs)
        return call

    def wait(self, sem, val=None):
        v = sem.n if val is None else val
        def op(eng):
            eng.wait_ge(sem.h, v)
        self.ops.append(op)
        self.kinds.append("wait")

    def sem_inc(self, sem, k=1):
        sem.n += k
        def op(eng):
            eng.nop().then_inc(sem.h, k)
        self.ops.append(op)
        self.kinds.append("nop")


def dinc(sem):
    return [(sem, 16)]


def inc1(sem, k=1):
    return [(sem, k)]


class Prog:
    def __init__(self, nc):
        self.nc = nc
        self.q = Eng("sync")
        self.t = Eng("tensor")
        self.v = Eng("vector")
        self.s = Eng("scalar")
        self.g = Eng("gpsimd")
        self._sem_guards = []

    def sem(self, name):
        guard = self.nc.semaphore(name)
        h = guard.__enter__()
        self._sem_guards.append(guard)
        return Sem(h)

    def finalize(self):
        nc = self.nc
        with nc.Block() as block:
            @block.sync
            def _(e):
                for op in self.q.ops:
                    op(e)

            @block.tensor
            def _(e):
                for op in self.t.ops:
                    op(e)

            @block.vector
            def _(e):
                for op in self.v.ops:
                    op(e)

            @block.scalar
            def _(e):
                for op in self.s.ops:
                    op(e)

            @block.gpsimd
            def _(e):
                for op in self.g.ops:
                    op(e)
        for gd in self._sem_guards:
            gd.__exit__(None, None, None)


def ts(i, n):
    return slice(i * n, (i + 1) * n)

def chunked(dram2d):
    return dram2d.rearrange("(c p) m -> p c m", p=128)


def sb3(sbuf_ap, m):
    return sbuf_ap.rearrange("p (c m) -> p c m", m=m)



PHASE_MARKS = []


def build_program(stage=99, dbg=()):
    nc = bass.Bass("TRN2", target_bir_lowering=False, debug=False)
    p = Prog(nc)
    PHASE_MARKS.clear()

    def mark(name):
        PHASE_MARKS.append((name, len(p.t.ops)))

    def din(name, shape, dtype=F32):
        return nc.dram_tensor(name, shape, dtype, kind="ExternalInput").ap()

    def dout(name, shape, dtype=F32):
        return nc.dram_tensor(name, shape, dtype, kind="ExternalOutput").ap()

    def dint(name, shape, dtype=F32):
        return nc.dram_tensor(name, shape, dtype, kind="Internal").ap()

    x_in = din("x_in", [2048, 1024])
    wqp = din("wqp", [4096, 1024])
    wkp = din("wkp", [1024, 4096])
    wvp = din("wvp", [1024, 4096])
    wo0 = din("wo0", [2048, 2048])
    bo_t = din("bo_t", [128, 16])
    wg8p = din("wg8p", [4096, 4096], I8)
    wu8p = din("wu8p", [4096, 4096], I8)
    wdp = din("wdp", [4096, 4096], I8)
    sg_t = din("sg_t", [128, 64]); su_t = din("su_t", [128, 64])
    sd_t = din("sd_t", [128, 16])
    wqd = din("wqd", [2048, 1536]); wkvd = din("wkvd", [2048, 512])
    wqwp = din("wqwp", [2048, 1152])
    wkwp = din("wkwp", [1024, 768])
    wvwp = din("wvwp", [1024, 768])
    wo1 = din("wo1", [3072, 2048])
    w1p = din("w1p", [5632, 2048])
    w2p = din("w2p", [5632, 2048])
    w3p = din("w3p", [5632, 2048])
    tok_off = din("tok_off", [1, 1], mybir.dt.int32)
    ident_bf = din("ident_bf", [128, 128], BF16)
    ones_f = din("ones_f", [1, 128])
    cosd = din("cosd", [64, 2048]); sind = din("sind", [64, 2048])
    mask_e = din("mask_e", [128, 256], BF16)
    mask_o = din("mask_o", [128, 256], BF16)

    y = dout("y", [2048, 1024])

    kvred_in = dint("kvred_in", [64, 2080])
    kvred_out = dint("kvred_out", [64, 2080])
    aout0_hi = dint("aout0_hi", [2048, 1024], BF16)
    aout0_lo = dint("aout0_lo", [2048, 1024], BF16)
    fuse_hi = dint("fuse_hi", [8192, 1024], BF16)
    fuse_lo = dint("fuse_lo", [8192, 1024], BF16)
    lat_in = dint("lat_in", [2048, 1024])
    lat_out = dint("lat_out", [4096, 1024])
    hid_hi = dint("hid_hi", [5632, 1024], BF16)
    hid_lo = dint("hid_lo", [5632, 1024], BF16)
    aout1 = dint("aout1", [1536, 2048])
    aout1_out = dint("aout1_out", [3072, 2048])

    def sb(name, shape, dtype):
        return nc.alloc_sbuf_tensor(name, shape, dtype).ap()

    sb_ident = sb("sb_ident", [128, 128], BF16)
    sb_cst = sb("sb_cst", [128, 4], F32)
    sb_ones = sb("sb_ones", [1, 128], F32)
    sb_onescol = sb("sb_onescol", [128, 1], F32)
    sb_bo = sb("sb_bo", [128, 16], F32)
    sb_sg = sb("sb_sg", [128, 64], F32)
    sb_su = sb("sb_su", [128, 64], F32)
    sb_sd = sb("sb_sd", [128, 16], F32)
    sb_maske = sb("sb_maske", [128, 256], BF16)
    sb_masko = sb("sb_masko", [128, 256], BF16)
    sb_cos = sb("sb_cos", [64, 2048], F32)
    sb_sin = sb("sb_sin", [64, 2048], F32)
    sb_thr = sb("sb_thr", [128, 8], F32)
    sb_gm = sb("sb_gm", [128, 8], F32)
    sb_gm2 = sb("sb_gm2", [128, 132], F32)
    sb_sm = sb("sb_sm", [128, 8], F32)
    off_sb = sb("off_sb", [1, 1], mybir.dt.int32)
    x_res = sb("x_res", [128, 16 * 1024], F32)
    xn_hi = sb("xn_hi", [128, 16 * 1024], BF16)
    xn_lo = sb("xn_lo", [128, 16 * 1024], BF16)
    K = 1024
    ARENA = (nc.sbuf_bytes_remaining - 384) // 64 * 64
    assert ARENA >= 57 * K, ARENA
    arena_t = nc.alloc_sbuf_tensor("arena", [128, ARENA], mybir.dt.int8)
    PH = None
    for a in nc.m.functions[0].allocations:
        if (isinstance(a, mybir.MemoryLocationSet) and a.memorylocations
                and a.memorylocations[0].name == arena_t.name):
            PH = a.memorylocations[0].addr
    assert PH is not None

    def addr_of(t):
        for a in nc.m.functions[0].allocations:
            if (isinstance(a, mybir.MemoryLocationSet) and a.memorylocations
                    and a.memorylocations[0].name == t.name):
                return a.memorylocations[0].addr
        raise KeyError(t.name)

    XO = addr_of(xn_hi.tensor)
    assert addr_of(xn_lo.tensor) == XO + 32768, "xn tiles not contiguous"

    def sb_at(name, shape, dtype, off):
        nbytes = int(np.prod(shape[1:])) * mybir.dt.size(dtype)
        inA = off >= PH and (off - PH) + nbytes <= ARENA
        inX = off >= XO and (off - XO) + nbytes <= 65536
        assert inA or inX, (name, off)
        return nc.alloc_sbuf_tensor_at(name, shape, dtype, offset=off).ap()

    scr = sb_at("scr", [128, 1024], F32, PH)
    scr2 = sb_at("scr2", [128, 1024], F32, PH + 4 * K)
    ab_b = sb_at("ab_b", [128, 1024], F32, PH + 8 * K)
    bb_b = sb_at("bb_b", [128, 1024], F32, PH + 12 * K)
    ln_t = sb_at("ln_t", [1, 4096], F32, PH + 16 * K)

    kv_wkh = sb_at("kv_wkh", [128, 4096], BF16, PH)
    kv_wkl = sb_at("kv_wkl", [128, 4096], BF16, PH + 8 * K)
    kv_wvh = sb_at("kv_wvh", [128, 4096], BF16, PH + 16 * K)
    kv_wvl = sb_at("kv_wvl", [128, 4096], BF16, PH + 24 * K)
    kv_wstg = [sb_at(f"kv_wstg{i}", [128, 2048], F32, PH + 32 * K + i * 8 * K)
               for i in range(2)]
    kq_hi = sb_at("kq_hi", [128, 256], BF16, PH + 56 * K + 512)
    kq_lo = sb_at("kq_lo", [128, 256], BF16, PH + 57 * K)
    vq_hi = sb_at("vq_hi", [128, 260], BF16, PH + 57 * K + 512)
    vq_lo = sb_at("vq_lo", [128, 260], BF16, PH + 58 * K + 64)
    kv_sb = sb_at("kv_sb", [64, 2080], F32, PH + 48 * K)

    q_wstg = [sb_at(f"q_wstg{i}", [128, 1024], F32, PH + i * 4 * K)
              for i in range(2)]
    q_wh = [sb_at(f"q_wh{i}", [128, 1024], BF16, PH + 8 * K + i * 4 * K)
            for i in range(2)]
    q_wl = [sb_at(f"q_wl{i}", [128, 1024], BF16, PH + 10 * K + i * 4 * K)
            for i in range(2)]
    q_hi = sb_at("q_hi", [64, 1024], BF16, PH + 16 * K)
    q_lo = sb_at("q_lo", [64, 1024], BF16, PH + 18 * K)
    r_sb = sb_at("r_sb", [1, 1024], F32, PH + 20 * K)
    rb_sb = sb_at("rb_sb", [64, 1024], F32, PH + 24 * K)
    oq_t = sb_at("oq_t", [64, 1024], F32, PH + 28 * K)
    out_hi = sb_at("out_hi", [64, 1024], BF16, PH + 32 * K)
    out_lo = sb_at("out_lo", [64, 1024], BF16, PH + 34 * K)
    kv_hi = sb_at("kv_hi", [64, 2080], BF16, PH + 36 * K)
    kv_lo = sb_at("kv_lo", [64, 2080], BF16, PH + 41 * K)

    wo_stg = [sb_at(f"wo_stg{i}", [128, 512], F32, PH + i * 2 * K)
              for i in range(2)]
    wo_hi = [sb_at(f"wo_hi{i}", [128, 512], BF16, PH + 4 * K + i * K)
             for i in range(2)]
    wo_lo = [sb_at(f"wo_lo{i}", [128, 512], BF16, PH + 6 * K + i * K)
             for i in range(2)]
    wo_rhi = [sb_at(f"wo_rhi{i}", [128, 1024], BF16, PH + 8 * K + i * 2 * K)
              for i in range(2)]
    wo_rlo = [sb_at(f"wo_rlo{i}", [128, 1024], BF16, PH + 12 * K + i * 2 * K)
              for i in range(2)]

    gu_wg = [sb_at(f"gu_wg{i}", [128, 4096], BF16, PH + i * 8 * K)
             for i in range(2)]
    gu_wu = [sb_at(f"gu_wu{i}", [128, 4096], BF16, PH + 16 * K + i * 8 * K)
             for i in range(2)]
    gu_g = sb_at("gu_g", [128, 1024], F32, PH + 32 * K)
    gu_u = sb_at("gu_u", [128, 1024], F32, PH + 36 * K)
    gu_f = sb_at("gu_f", [128, 1024], F32, PH + 40 * K)
    gu_fh = [sb_at(f"gu_fh{i}", [128, 1024], BF16, PH + 44 * K + i * 2 * K)
             for i in range(2)]
    gu_fl = [sb_at(f"gu_fl{i}", [128, 1024], BF16, PH + 48 * K + i * 2 * K)
             for i in range(2)]

    dn_w = [sb_at(f"dn_w{i}", [128, 4096], BF16, PH + 16 * K + i * 8 * K)
            for i in range(2)]
    dn_fh = [sb_at(f"dn_fh{i}", [128, 2048], BF16, PH + 32 * K + i * 4 * K)
             for i in range(2)]
    dn_fl = [sb_at(f"dn_fl{i}", [128, 2048], BF16, PH + 40 * K + i * 4 * K)
             for i in range(2)]

    lt_st = [sb_at(f"lt_st{i}", [128, 1024], F32, PH + 16 * K + i * 4 * K)
             for i in range(2)]
    gm_slab = [sb_at(f"gm_slab{i}", [128, 2048], F32, PH + 28 * K + i * 8 * K)
               for i in range(2)]
    dn3_stg = [sb_at(f"dn3_stg{i}", [128, 2048], F32, PH + i * 8 * K)
               for i in range(2)]
    w3_tmp2 = sb_at("w3_tmp2", [128, 2048], BF16, PH + 48 * K)
    xw1_stg = sb_at("xw1_stg", [128, 2048], F32, PH)
    xw2_stg = sb_at("xw2_stg", [128, 2048], F32, PH + 8 * K)
    xw_tmp = sb_at("xw_tmp", [128, 2048], BF16, PH + 16 * K)
    xw_tmp2 = sb_at("xw_tmp2", [128, 2048], BF16, PH + 20 * K)
    q1s = [sb_at(f"q1s{i}", [128, 2048], BF16, PH + 24 * K + i * 4 * K)
           for i in range(2)]
    q2s = [sb_at(f"q2s{i}", [128, 2048], BF16, PH + 32 * K + i * 4 * K)
           for i in range(2)]
    hd_s = sb_at("hd_s", [128, 1024], F32, PH + 40 * K)
    hd_f = sb_at("hd_f", [128, 1024], F32, PH + 44 * K)
    hd_h = [sb_at(f"hd_h{i}", [128, 1024], BF16, PH + 48 * K + i * 2 * K)
            for i in range(2)]
    hd_l = [sb_at(f"hd_l{i}", [128, 1024], BF16, PH + 52 * K + i * 2 * K)
            for i in range(2)]
    kvl_hi = sb_at("kvl_hi", [128, 4 * 2048], BF16, PH)
    kvl_lo = sb_at("kvl_lo", [128, 4 * 2048], BF16, PH + 16 * K)
    kvstg = sb_at("kvstg", [128, 2048], F32, PH + 32 * K)
    ql_stg = [sb_at(f"ql_stg{i}", [128, 1024], F32, PH + 32 * K + i * 8 * K)
              for i in range(2)]
    ql_hi = [sb_at(f"ql_hi{i}", [128, 1024], BF16, PH + 36 * K + i * 8 * K)
             for i in range(2)]
    ql_lo = [sb_at(f"ql_lo{i}", [128, 1024], BF16, PH + 38 * K + i * 8 * K)
             for i in range(2)]
    qw_stg = sb_at("qw_stg", [128, 1152], F32, PH + 48 * K)
    qw_hi = [sb_at(f"qw_hi{i}", [128, 1152], BF16, PH + 52 * K + 512 + i * 2304)
             for i in range(2)]
    qw_lo = [sb_at(f"qw_lo{i}", [128, 1152], BF16, XO + 48 * K + i * 2304)
             for i in range(2)]
    kn_hi = sb_at("kn_hi", [128, 2048], BF16, PH + 32 * K)
    kn_lo = sb_at("kn_lo", [128, 2048], BF16, PH + 36 * K)
    av_hi = sb_at("av_hi", [128, 3072], BF16, PH + 40 * K)
    av_lo = sb_at("av_lo", [128, 3072], BF16, PH + 46 * K)
    rsc1 = sb_at("rsc1", [64, 512], F32, XO + 58 * K)
    rsc2 = sb_at("rsc2", [64, 512], F32, XO + 60 * K)
    rsc3 = sb_at("rsc3", [64, 512], F32, XO + 62 * K)
    ptr_hi = sb_at("ptr_hi", [128, 512], BF16, PH + 56 * K)
    ptr_lo = sb_at("ptr_lo", [128, 512], BF16, PH + 57 * K)
    pv_st = sb_at("pv_st", [128, 512], F32, PH + 58 * K)
    qn_hi = sb_at("qn_hi", [128, 2048], BF16, XO)
    qn_lo = sb_at("qn_lo", [128, 2048], BF16, XO + 4 * K)
    qr_hi = sb_at("qr_hi", [64, 2048], BF16, XO + 8 * K)
    qr_lo = sb_at("qr_lo", [64, 2048], BF16, XO + 12 * K)
    kr_hi = sb_at("kr_hi", [64, 2048], BF16, XO + 16 * K)
    kr_lo = sb_at("kr_lo", [64, 2048], BF16, XO + 20 * K)
    p_f = sb_at("p_f", [128, 2048], F32, XO + 24 * K)
    ph_p = sb_at("ph_p", [128, 4096], BF16, XO + 32 * K)
    pl_p = sb_at("pl_p", [128, 4096], BF16, XO + 40 * K)
    kw_stg = sb_at("kw_stg", [128, 768], F32, XO + 52 * K)
    kw_hi = sb_at("kw_hi", [128, 768], BF16, XO + 55 * K)
    kw_lo = sb_at("kw_lo", [128, 768], BF16, XO + 56 * K + 512)
    vw_stg = sb_at("vw_stg", [128, 768], F32, XO + 52 * K)
    vw_hi = sb_at("vw_hi", [128, 768], BF16, XO + 55 * K)
    vw_lo = sb_at("vw_lo", [128, 768], BF16, XO + 56 * K + 512)
    w1r_stg = [sb_at(f"w1r_stg{i}", [128, 1024], F32, PH + 16 * K + i * 4 * K)
               for i in range(2)]
    w1r_hi = [sb_at(f"w1r_hi{i}", [128, 1024], BF16, PH + 24 * K + i * 2 * K)
              for i in range(2)]
    w1r_lo = [sb_at(f"w1r_lo{i}", [128, 1024], BF16, PH + 28 * K + i * 2 * K)
              for i in range(2)]

    psA_t = nc.alloc_psum_tensor("psA", [128, 2048], F32)
    psA = psA_t.ap()
    pbankB = [nc.alloc_psum_tensor(f"pb{i}", [128, 512], F32) for i in range(4, 8)]
    pbank = [psA[:, i * 512:(i + 1) * 512] for i in range(4)] + \
            [t.ap() for t in pbankB]
    pt_ps = pbankB[2].bitcast(BF16).ap()
    pt_ps2 = pbankB[3].bitcast(BF16).ap()

    s_pro = p.sem("s_pro")
    s_mm = p.sem("s_mm")
    s_v = p.sem("s_v")
    s_s = p.sem("s_s")
    s_free = p.sem("s_free")
    s_cc = p.sem("s_cc")
    s_st = p.sem("s_st")
    s_kwh = p.sem("s_kwh")
    s_kstg = p.sem("s_kstg")
    s_kvin = p.sem("s_kvin")
    s_kvout = p.sem("s_kvout")
    s_guw = [p.sem("s_guw0"), p.sem("s_guw1")]
    s_fu = [p.sem("s_fu0"), p.sem("s_fu1")]
    s_qwa = [p.sem("s_qwa0"), p.sem("s_qwa1")]
    s_qwb = [p.sem("s_qwb0"), p.sem("s_qwb1")]
    s_ao = p.sem("s_ao")
    s_woa = [p.sem("s_woa0"), p.sem("s_woa1")]
    s_wob = [p.sem("s_wob0"), p.sem("s_wob1")]
    s_dn = [p.sem("s_dn0"), p.sem("s_dn1")]
    s_dbg = p.sem("s_dbg")
    s_lt = [p.sem("s_lt0"), p.sem("s_lt1")]
    s_ltb = [p.sem("s_ltb0"), p.sem("s_ltb1")]
    s_lte = [p.sem("s_lte0"), p.sem("s_lte1")]
    s_gm = [p.sem("s_gm0"), p.sem("s_gm1")]
    s_w3s = [p.sem("s_w3s0"), p.sem("s_w3s1")]
    s_x1w = [p.sem("s_x1w0"), p.sem("s_x1w1")]
    s_hid = [p.sem("s_hid0"), p.sem("s_hid1")]
    s_kvla = p.sem("s_kvla")
    s_kvlb = p.sem("s_kvlb")
    s_qla = p.sem("s_qla")
    s_qla2 = [p.sem("s_qla2_0"), p.sem("s_qla2_1")]
    s_qlb2 = [p.sem("s_qlb2_0"), p.sem("s_qlb2_1")]
    s_qwsa = [p.sem("s_qwsa0"), p.sem("s_qwsa1")]
    s_qwsb = [p.sem("s_qwsb0"), p.sem("s_qwsb1")]
    s_kw = p.sem("s_kw")
    s_kwb = p.sem("s_kwb")
    s_vw = p.sem("s_vw")
    s_vwb = p.sem("s_vwb")
    s_av = p.sem("s_av")
    s_dn1 = [p.sem("s_dn1_0"), p.sem("s_dn1_1")]
    s_w1r = [p.sem("s_w1r0"), p.sem("s_w1r1")]

    q_, t_, v_, s_, g_ = p.q, p.t, p.v, p.s, p.g

    for (dstt, srct) in ((sb_ident, ident_bf), (sb_ones, ones_f), (sb_bo, bo_t),
                         (sb_sg, sg_t), (sb_su, su_t), (sb_sd, sd_t),
                         (sb_maske, mask_e), (sb_masko, mask_o),
                         (sb_cos, cosd), (sb_sin, sind)):
        q_.dma_start(out=dstt[:, :], in_=srct[:, :], inc=dinc(s_pro))
    s_.dma_start(out=sb3(x_res[:, :], 1024), in_=chunked(x_in), inc=dinc(s_pro))
    v_.memset(sb_onescol[:, :], 1.0)
    v_.memset(sb_cst[:, 0:1], 1e-5)
    v_.memset(vq_hi.rearrange("p (h c) -> p h c", c=65)[:, :, 64:65], 1.0)
    v_.memset(vq_lo.rearrange("p (h c) -> p h c", c=65)[:, :, 64:65], 0.0,
              inc=inc1(s_v))

    def layernorm(src_tile, dst_hi, dst_lo):
        t_.wait(s_pro)
        t_.wait(s_v)
        s_.wait(s_pro)
        s_.wait(s_v)
        mm_base = s_mm.n
        ss_base = s_s.n
        for ic in range(NCH):
            buf = scr if ic % 2 == 0 else scr2
            if ic >= 2:
                s_.wait(s_mm, mm_base + (ic - 1))
            s_.activation(out=buf[:, :], in_=src_tile[:, ts(ic, 1024)],
                          func=AF.Square, inc=inc1(s_s))
            t_.wait(s_s, ss_base + ic + 1)
            st = (ic == 0); sp = (ic == NCH - 1)
            for tb in range(2):
                xs = src_tile[:, ic * 1024 + tb * 512: ic * 1024 + (tb + 1) * 512]
                t_.matmul(pbank[4 + tb][0:1, :], sb_onescol[:, :], xs,
                          start=st, stop=sp)
                t_.matmul(pbank[6 + tb][0:1, :], sb_onescol[:, :], buf[:, ts(tb, 512)],
                          start=st, stop=sp,
                          inc=inc1(s_mm) if tb == 1 else None)
        m = ln_t[0:1, 0:1024]; msq = ln_t[0:1, 1024:2048]
        va = ln_t[0:1, 2048:3072]; rinv = ln_t[0:1, 3072:4096]
        v_.wait(s_mm)
        for tb in range(2):
            v_.tensor_scalar_mul(m[:, ts(tb, 512)], pbank[4 + tb][0:1, :],
                                 1.0 / 2048.0, inc=inc1(s_v) if tb == 1 else None)
        s_.wait(s_v)
        s_.activation(out=msq, in_=m, func=AF.Square, inc=inc1(s_s))
        v_.wait(s_s)
        for tb in range(2):
            v_.scalar_tensor_tensor(va[:, ts(tb, 512)], pbank[6 + tb][0:1, :],
                                    1.0 / 2048.0, msq[:, ts(tb, 512)],
                                    OP.mult, OP.subtract,
                                    inc=inc1(s_v) if tb == 1 else None)
        s_.wait(s_v)
        s_.activation(out=rinv, in_=va, func=AF.Sqrt, bias=sb_cst[0:1, 0:1],
                      inc=inc1(s_s))
        v_.wait(s_s)
        v_.reciprocal(rinv, rinv)
        v_.drain()
        v_.tensor_tensor(va, m, rinv, OP.mult)
        v_.drain()
        v_.tensor_scalar_mul(va, va, -1.0, inc=inc1(s_v))
        t_.wait(s_v)
        for tb in range(2):
            t_.matmul(pbank[0 + tb][:, :], sb_ones[:, :], rinv[:, ts(tb, 512)],
                      start=True, stop=True)
            t_.matmul(pbank[2 + tb][:, :], sb_ones[:, :], va[:, ts(tb, 512)],
                      start=True, stop=True,
                      inc=inc1(s_mm) if tb == 1 else None)
        v_.wait(s_mm)
        for tb in range(2):
            v_.tensor_copy(out=ab_b[:, ts(tb, 512)], in_=pbank[0 + tb][:, :])
            v_.tensor_copy(out=bb_b[:, ts(tb, 512)], in_=pbank[2 + tb][:, :])
        v_.drain()
        for ic in range(NCH):
            v_.tensor_tensor(scr[:, :], src_tile[:, ts(ic, 1024)], ab_b[:, :], OP.mult)
            v_.drain()
            v_.tensor_tensor(scr2[:, :], scr[:, :], bb_b[:, :], OP.add, inc=inc1(s_v))
            s_.wait(s_v)
            s_.copy(out=dst_hi[:, ts(ic, 1024)], in_=scr2[:, :], inc=inc1(s_s))
            v_.wait(s_s)
            v_.tensor_tensor(dst_lo[:, ts(ic, 1024)], scr2[:, :],
                             dst_hi[:, ts(ic, 1024)], OP.subtract, inc=inc1(s_v))

    mark('LN0')
    layernorm(x_res, xn_hi, xn_lo)

    def xn_piece(hi, ic, tkslice):
        src = xn_hi if hi else xn_lo
        return src[:, ic * 1024 + tkslice.start: ic * 1024 + tkslice.stop]

    mark('kv')
    if stage >= 1:
        kv_stg_marks = {}
        for e8 in range(8):
            rsl = slice(e8 * 128, (e8 + 1) * 128)
            g_.wait(s_mm)
            g_.wait(s_v)
            g_.dma_start(out=kv_wkh[:, :], in_=wkp[rsl, :], inc=dinc(s_kwh))
            g_.dma_start(out=kv_wvh[:, :], in_=wvp[rsl, :], inc=dinc(s_kwh))
            kwh_done = s_kwh.n
            kfills = [(whi, wlo, hf2)
                      for (whi, wlo) in ((kv_wkh, kv_wkl), (kv_wvh, kv_wvl))
                      for hf2 in range(2)]
            for fi, (whi, wlo, hf2) in enumerate(kfills):
                sbuf_ = fi % 2
                wsrc = wkp if whi is kv_wkh else wvp
                if sbuf_ in kv_stg_marks:
                    q_.wait(s_v, kv_stg_marks[sbuf_])
                q_.dma_start(
                    out=kv_wstg[sbuf_][:, :],
                    in_=wsrc[rsl, hf2 * 2048:(hf2 + 1) * 2048],
                    inc=dinc(s_kstg) if sbuf_ == 0 else dinc(s_kvin))
                v_.wait(s_kstg if sbuf_ == 0 else s_kvin)
                v_.wait(s_kwh, kwh_done)
                v_.tensor_tensor(wlo[:, hf2 * 2048:(hf2 + 1) * 2048],
                                 kv_wstg[sbuf_][:, :],
                                 whi[:, hf2 * 2048:(hf2 + 1) * 2048],
                                 OP.subtract, inc=inc1(s_v))
                kv_stg_marks[sbuf_] = s_v.n
            t_.wait(s_v)
            for tc in range(8):
                tk = slice(tc * 128, (tc + 1) * 128)
                for kc in range(NCH):
                    wkh = kv_wkh[:, ts(kc, 256)]; wkl = kv_wkl[:, ts(kc, 256)]
                    wvh = kv_wvh[:, ts(kc, 256)]; wvl = kv_wvl[:, ts(kc, 256)]
                    xh = xn_piece(True, kc, tk); xl = xn_piece(False, kc, tk)
                    st = (kc == 0); sp = (kc == NCH - 1)
                    t_.matmul(pbank[0][:, 0:256], xh, wkh, start=st, stop=False)
                    t_.matmul(pbank[0][:, 0:256], xh, wkl, start=False, stop=False)
                    t_.matmul(pbank[1][:, 0:256], xh, wvh, start=st, stop=False)
                    t_.matmul(pbank[1][:, 0:256], xh, wvl, start=False, stop=False)
                    t_.matmul(pbank[0][:, 0:256], xl, wkh, start=False, stop=sp)
                    t_.matmul(pbank[1][:, 0:256], xl, wvh, start=False, stop=sp,
                              inc=inc1(s_mm) if sp else None)
                s_.wait(s_mm)
                s_.activation(out=kq_hi[:, :], in_=pbank[0][:, 0:256], func=AF.Relu,
                              inc=inc1(s_s))
                v_.wait(s_s)
                v_.scalar_tensor_tensor(kq_lo[:, :], pbank[0][:, 0:256], 0.0,
                                        kq_hi[:, :], OP.max, OP.subtract)
                vd_h = vq_hi.rearrange("p (h c) -> p h c", c=65)[:, :, 0:64]
                vd_l = vq_lo.rearrange("p (h c) -> p h c", c=65)[:, :, 0:64]
                s_.copy(out=vd_h,
                        in_=pbank[1][:, 0:256].rearrange("p (h c) -> p h c", c=64),
                        inc=inc1(s_s))
                v_.wait(s_s)
                v_.tensor_tensor(vd_l,
                                 pbank[1][:, 0:256].rearrange("p (h c) -> p h c", c=64),
                                 vd_h, OP.subtract, inc=inc1(s_v))
                t_.wait(s_v)
                for hh in range(4):
                    kh = kq_hi[:, ts(hh, 64)]; kl = kq_lo[:, ts(hh, 64)]
                    vh = vq_hi[:, hh * 65:(hh + 1) * 65]
                    vl = vq_lo[:, hh * 65:(hh + 1) * 65]
                    od = pbank[2][0:64, hh * 65:(hh + 1) * 65]
                    t_.matmul(od, kh, vh, start=(tc == 0 and hh == 0), stop=False)
                    t_.matmul(od, kh, vl, start=False, stop=False)
                    t_.matmul(od, kl, vh, start=False,
                              stop=(tc == 7 and hh == 3),
                              inc=inc1(s_mm) if (tc == 7 and hh == 3) else None)
            v_.wait(s_mm)
            v_.tensor_copy(out=kv_sb[:, e8 * 260:(e8 + 1) * 260],
                           in_=pbank[2][0:64, 0:260], inc=inc1(s_v))
        q_.wait(s_v)
        q_.dma_start(out=kvred_in[:, :], in_=kv_sb[:, :], inc=dinc(s_kvin))
        g_.wait(s_kvin)
        g_.collective_compute(
            "AllReduce", OP.add,
            replica_groups=[[0, 1], [2, 3], [4, 5], [6, 7]],
            ins=[kvred_in[:, :]], outs=[kvred_out[:, :]], inc=inc1(s_cc))

    mark('gateup')
    fuse_done = [0, 0]
    fu_marks = {}
    gu_ss_marks = [0, 0]
    if stage >= 2:
        mm_marks = []
        for mg in range(32):
            buf = mg % 2
            if mg >= 2:
                g_.wait(s_mm, mm_marks[mg - 2])
            else:
                g_.wait(s_v)
                g_.wait(s_mm)
            g_.dma_start(out=gu_wg[buf][:, :], in_=wg8p[ts(mg, 128), :],
                         inc=dinc(s_guw[buf]))
            g_.dma_start(out=gu_wu[buf][:, :], in_=wu8p[ts(mg, 128), :],
                         inc=dinc(s_guw[buf]))
            t_.wait(s_guw[buf])
            if mg == 0:
                t_.wait(s_s)
            mc_mm_marks = []
            for mc in range(2):
                if gu_ss_marks[mc]:
                    t_.wait(s_s, gu_ss_marks[mc])
                for kc in range(NCH):
                    st = (kc == 0); sp = (kc == NCH - 1)
                    wgt = gu_wg[buf][:, kc * 256 + mc * 128: kc * 256 + (mc + 1) * 128]
                    wut = gu_wu[buf][:, kc * 256 + mc * 128: kc * 256 + (mc + 1) * 128]
                    for tb in range(2):
                        xh = xn_piece(True, kc, ts(tb, 512))
                        xl = xn_piece(False, kc, ts(tb, 512))
                        pg = pbank[mc * 2 + tb]
                        pu = pbank[4 + mc * 2 + tb]
                        t_.matmul(pg[:, :], wgt, xh, start=st, stop=False)
                        t_.matmul(pg[:, :], wgt, xl, start=False, stop=sp)
                        t_.matmul(pu[:, :], wut, xh, start=st, stop=False)
                        t_.matmul(pu[:, :], wut, xl, start=False, stop=sp,
                                  inc=inc1(s_mm)
                                  if (sp and tb == 1) else None)
                mc_mm_marks.append(s_mm.n)
            mm_marks.append(s_mm.n)
            for mc in range(2):
                mabs = mg * 2 + mc
                fb = mabs % 2
                sga = sb_sg[:, mabs:mabs + 1]
                sua = sb_su[:, mabs:mabs + 1]
                s_.wait(s_mm, mc_mm_marks[mc])
                s_.wait(s_v)
                for tb in range(2):
                    s_.activation(out=gu_g[:, ts(tb, 512)], in_=pbank[mc * 2 + tb][:, :],
                                  func=AF.Silu, scale=sga)
                    s_.activation(out=gu_u[:, ts(tb, 512)], in_=pbank[4 + mc * 2 + tb][:, :],
                                  func=AF.Copy, scale=sua, inc=inc1(s_s))
                gu_ss_marks[mc] = s_s.n
                v_.wait(s_s)
                v_.tensor_tensor(gu_f[:, :], gu_g[:, :], gu_u[:, :], OP.mult)
                if fuse_done[fb]:
                    v_.wait(s_fu[fb], fuse_done[fb])
                v_.drain()
                v_.tensor_copy(out=gu_fh[fb][:, :], in_=gu_f[:, :])
                v_.drain()
                v_.tensor_tensor(gu_fl[fb][:, :], gu_f[:, :], gu_fh[fb][:, :],
                                 OP.subtract, inc=inc1(s_v))
                q_.wait(s_v)
                q_.dma_start(out=fuse_hi[ts(mabs, 128), :], in_=gu_fh[fb][:, :],
                             inc=dinc(s_fu[fb]))
                q_.dma_start(out=fuse_lo[ts(mabs, 128), :], in_=gu_fl[fb][:, :],
                             inc=dinc(s_fu[fb]))
                fuse_done[fb] = s_fu[fb].n
                fu_marks[mabs] = s_fu[fb].n

    mark('qapply')
    aout_done = 0
    if stage >= 1:
        g_.wait(s_cc)
        g_.wait(s_mm)
        g_.wait(s_v)
        if fuse_done[0]:
            g_.wait(s_fu[0], fuse_done[0])
            g_.wait(s_fu[1], fuse_done[1])
        g_.dma_start(out=kv_sb[:, :], in_=kvred_out[:, :], inc=dinc(s_kvout))
        s_.wait(s_kvout)
        s_.copy(out=kv_hi[:, :], in_=kv_sb[:, :], inc=inc1(s_s))
        v_.wait(s_s)
        v_.tensor_tensor(kv_lo[:, :], kv_sb[:, :], kv_hi[:, :], OP.subtract,
                         inc=inc1(s_v))
        mm_marks = []
        lo_marks = []

        def qfill(h):
            buf = h % 2
            if h >= 2:
                q_.wait(s_mm, mm_marks[h - 2])
                g_.wait(s_mm, mm_marks[h - 2])
            else:
                q_.wait(s_mm)
                g_.wait(s_mm)
            q_.dma_start(out=q_wstg[buf][:, :], in_=wqp[ts(h, 128), :],
                         inc=dinc(s_qwa[buf]))
            g_.dma_start(out=q_wh[buf][:, :], in_=wqp[ts(h, 128), :],
                         inc=dinc(s_qwb[buf]))
            v_.wait(s_qwa[buf])
            v_.wait(s_qwb[buf])
            v_.drain()
            v_.tensor_tensor(q_wl[buf][:, :], q_wstg[buf][:, :], q_wh[buf][:, :],
                             OP.subtract, inc=inc1(s_v))
            lo_marks.append(s_v.n)

        qfill(0)
        prev_qsplit = 0
        for h in range(32):
            buf = h % 2
            t_.wait(s_v, max(lo_marks[h], prev_qsplit))
            for kc in range(NCH):
                st = (kc == 0); sp = (kc == NCH - 1)
                wht = q_wh[buf][:, ts(kc, 64)]; wlt = q_wl[buf][:, ts(kc, 64)]
                for tb in range(2):
                    xh = xn_piece(True, kc, ts(tb, 512))
                    xl = xn_piece(False, kc, ts(tb, 512))
                    pq = pbank[tb][0:64, :]
                    t_.matmul(pq, wht, xh, start=st, stop=False)
                    t_.matmul(pq, wht, xl, start=False, stop=False)
                for tb in range(2):
                    xh = xn_piece(True, kc, ts(tb, 512))
                    pq = pbank[tb][0:64, :]
                    t_.matmul(pq, wlt, xh, start=False, stop=sp,
                              inc=inc1(s_mm)
                              if (sp and tb == 1) else None)
            mm_marks.append(s_mm.n)
            if h + 1 < 32:
                qfill(h + 1)
            s_.wait(s_mm, mm_marks[h])
            for tb in range(2):
                s_.activation(out=q_hi[:, ts(tb, 512)], in_=pbank[tb][0:64, :],
                              func=AF.Relu, inc=inc1(s_s) if tb == 1 else None)
            v_.wait(s_s)
            for tb in range(2):
                v_.scalar_tensor_tensor(q_lo[:, ts(tb, 512)], pbank[tb][0:64, :], 0.0,
                                        q_hi[:, ts(tb, 512)], OP.max, OP.subtract,
                                        inc=inc1(s_v) if tb == 1 else None)
            qsplit_mark = s_v.n
            prev_qsplit = qsplit_mark
            t_.wait(s_v, qsplit_mark)
            kvh = kv_hi[:, h * 65:h * 65 + 64]; kvl = kv_lo[:, h * 65:h * 65 + 64]
            ksh = kv_hi[:, h * 65 + 64:h * 65 + 65]
            ksl = kv_lo[:, h * 65 + 64:h * 65 + 65]
            for tb in range(2):
                qh = q_hi[:, ts(tb, 512)]; ql = q_lo[:, ts(tb, 512)]
                po = pbank[2 + tb][0:64, :]
                t_.matmul(po, kvh, qh, start=True, stop=False)
                t_.matmul(po, kvh, ql, start=False, stop=False)
                t_.matmul(po, kvl, qh, start=False, stop=True)
                pn = pbank[4 + tb][0:1, :]
                t_.matmul(pn, ksh, qh, start=True, stop=False)
                t_.matmul(pn, ksh, ql, start=False, stop=False)
                t_.matmul(pn, ksl, qh, start=False, stop=True,
                          inc=inc1(s_mm) if tb == 1 else None)
            qkv_mark = s_mm.n
            v_.wait(s_mm, qkv_mark)
            for tb in range(2):
                v_.tensor_scalar_max(r_sb[:, ts(tb, 512)], pbank[4 + tb][0:1, :],
                                     1e-6)
            v_.drain()
            v_.reciprocal(r_sb[:, :], r_sb[:, :], inc=inc1(s_v))
            t_.wait(s_v)
            for tb in range(2):
                t_.matmul(pbank[6 + tb][0:64, :], sb_ones[:, 0:64],
                          r_sb[:, ts(tb, 512)], start=True, stop=True,
                          inc=inc1(s_mm) if tb == 1 else None)
            v_.wait(s_mm)
            for tb in range(2):
                v_.tensor_copy(out=rb_sb[:, ts(tb, 512)], in_=pbank[6 + tb][0:64, :])
            v_.drain()
            for tb in range(2):
                v_.tensor_tensor(oq_t[:, ts(tb, 512)], pbank[2 + tb][0:64, :],
                                 rb_sb[:, ts(tb, 512)], OP.mult,
                                 inc=inc1(s_v) if tb == 1 else None)
            s_.wait(s_v)
            if h >= 1:
                s_.wait(s_ao, aout_done)
            s_.copy(out=out_hi[:, :], in_=oq_t[:, :], inc=inc1(s_s))
            v_.wait(s_s)
            if h >= 1:
                v_.wait(s_ao, aout_done)
            v_.tensor_tensor(out_lo[:, :], oq_t[:, :], out_hi[:, :], OP.subtract,
                             inc=inc1(s_v))
            q_.wait(s_v)
            q_.dma_start(out=aout0_hi[ts(h, 64), :], in_=out_hi[:, :],
                         inc=dinc(s_ao))
            q_.dma_start(out=aout0_lo[ts(h, 64), :], in_=out_lo[:, :],
                         inc=dinc(s_ao))
            aout_done = s_ao.n

    pre_wo0_mm = s_mm.n
    pre_wo0_v = s_v.n
    pre_wo0_ao = aout_done
    mark('wo0')
    if stage >= 3:
        mm_marks = []
        u = 0
        for mcg in range(4):
            t_.wait(s_v)
            for kc in range(NCH):
                buf = u % 2
                for eng in (q_, g_):
                    if u >= 2:
                        eng.wait(s_mm, mm_marks[u - 2])
                    else:
                        eng.wait(s_mm)
                        eng.wait(s_v)
                        eng.wait(s_ao, aout_done)
                q_.dma_start(out=wo_stg[buf][:, :],
                             in_=wo0[ts(kc, 128), ts(mcg, 512)], inc=dinc(s_woa[buf]))
                g_.dma_start(out=wo_hi[buf][:, :],
                             in_=wo0[ts(kc, 128), ts(mcg, 512)], inc=dinc(s_wob[buf]))
                g_.dma_start(out=wo_rhi[buf][:, :], in_=aout0_hi[ts(kc, 128), :],
                             inc=dinc(s_wob[buf]))
                g_.dma_start(out=wo_rlo[buf][:, :], in_=aout0_lo[ts(kc, 128), :],
                             inc=dinc(s_wob[buf]))
                v_.wait(s_woa[buf])
                v_.wait(s_wob[buf])
                v_.tensor_tensor(wo_lo[buf][:, :], wo_stg[buf][:, :],
                                 wo_hi[buf][:, :], OP.subtract, inc=inc1(s_v))
                t_.wait(s_v)
                st = (kc == 0); sp = (kc == NCH - 1)
                for mc in range(4):
                    wht = wo_hi[buf][:, ts(mc, 128)]; wlt = wo_lo[buf][:, ts(mc, 128)]
                    for tb in range(2):
                        rh = wo_rhi[buf][:, ts(tb, 512)]
                        rl = wo_rlo[buf][:, ts(tb, 512)]
                        po = pbank[mc * 2 + tb]
                        t_.matmul(po[:, :], wht, rh, start=st, stop=False)
                        t_.matmul(po[:, :], wht, rl, start=False, stop=False)
                    for tb in range(2):
                        rh = wo_rhi[buf][:, ts(tb, 512)]
                        po = pbank[mc * 2 + tb]
                        t_.matmul(po[:, :], wlt, rh, start=False, stop=sp,
                                  inc=inc1(s_mm)
                                  if (mc == 3 and tb == 1) else None)
                mm_marks.append(s_mm.n)
                u += 1
            v_.wait(s_mm)
            for mc in range(4):
                mabs = mcg * 4 + mc
                for tb in range(2):
                    xd = x_res[:, mabs * 1024 + tb * 512: mabs * 1024 + (tb + 1) * 512]
                    v_.scalar_tensor_tensor(xd, pbank[mc * 2 + tb][:, :],
                                            sb_bo[:, mabs:mabs + 1], xd,
                                            OP.add, OP.add,
                                            inc=inc1(s_v) if (mc == 3 and tb == 1) else None)

    pre_gm_v = s_v.n

    mark('down0')
    if stage >= 3:
        mm_marks = []
        ep_marks = []
        u = 0
        for tb in range(2):
            for mcg in range(2):
                t_.wait(s_v, ep_marks[-1] if ep_marks else pre_gm_v)
                for kcq in range(16):
                    buf = u % 2
                    if u >= 2:
                        g_.wait(s_mm, mm_marks[u - 2])
                    else:
                        g_.wait(s_mm, pre_wo0_mm)
                        g_.wait(s_v, pre_wo0_v)
                        g_.wait(s_ao, pre_wo0_ao)
                    g_.dma_start(out=dn_w[buf][:, :],
                                 in_=wdp[(mcg * 16 + kcq) * 128:
                                         (mcg * 16 + kcq + 1) * 128, :],
                                 inc=dinc(s_dn[buf]))
                    need_fu = [0, 0]
                    for kc4 in range(4):
                        kc = kcq * 4 + kc4
                        need_fu[kc % 2] = max(need_fu[kc % 2], fu_marks[kc])
                    g_.wait(s_fu[0], need_fu[0])
                    g_.wait(s_fu[1], need_fu[1])
                    for kc4 in range(4):
                        kc = kcq * 4 + kc4
                        g_.dma_start(out=dn_fh[buf][:, ts(kc4, 512)],
                                     in_=fuse_hi[ts(kc, 128), ts(tb, 512)],
                                     inc=dinc(s_dn[buf]))
                        g_.dma_start(out=dn_fl[buf][:, ts(kc4, 512)],
                                     in_=fuse_lo[ts(kc, 128), ts(tb, 512)],
                                     inc=dinc(s_dn[buf]))
                    t_.wait(s_dn[buf])
                    for kc4 in range(4):
                        st = (kcq == 0 and kc4 == 0); sp = (kcq == 15 and kc4 == 3)
                        for mc in range(8):
                            wt = dn_w[buf][:, kc4 * 1024 + mc * 128:
                                           kc4 * 1024 + (mc + 1) * 128]
                            po = pbank[mc]
                            t_.matmul(po[:, :], wt, dn_fh[buf][:, ts(kc4, 512)],
                                      start=st, stop=False)
                            t_.matmul(po[:, :], wt, dn_fl[buf][:, ts(kc4, 512)],
                                      start=False, stop=sp,
                                      inc=inc1(s_mm)
                                      if (kc4 == 3 and mc == 7) else None)
                    mm_marks.append(s_mm.n)
                    u += 1
                v_.wait(s_mm)
                for mc in range(8):
                    mabs = mcg * 8 + mc
                    xd = x_res[:, mabs * 1024 + tb * 512: mabs * 1024 + (tb + 1) * 512]
                    v_.scalar_tensor_tensor(xd, pbank[mc][:, :],
                                            sb_sd[:, mabs:mabs + 1], xd,
                                            OP.mult, OP.add,
                                            inc=inc1(s_v) if mc == 7 else None)
                ep_marks.append(s_v.n)

    CATT = 1.0 / float(np.sqrt(192.0))
    RG = [[0, 1], [2, 3], [4, 5], [6, 7]]
    tok_cell = {}
    if stage >= 4:
        q_.wait(s_pro)
        q_.dma_start(out=off_sb[:, :], in_=tok_off[:, :], inc=dinc(s_qla))

        def _load_tok(eng):
            eng.wait_ge(s_qla.h, 16)
            tok_cell["v"] = eng.value_load(off_sb[0:1, 0:1])
        q_.ops.append(_load_tok)

        mark('LN1')
        s_.drain()
        layernorm(x_res, xn_hi, xn_lo)

        gm_srcs = [(wi, j) for wi in range(3) for j in range(44)]
        gm_w = (w1p, w2p, w3p)
        s_.wait(s_v)
        s_.wait(s_mm)

        def gm_fill(idx):
            wi, j = gm_srcs[idx]
            s_.dma_start(out=gm_slab[idx % 2][:, :],
                         in_=gm_w[wi][j * 128:(j + 1) * 128, :],
                         inc=dinc(s_gm[idx % 2]))

        gm_fill(0)
        for idx in range(len(gm_srcs)):
            wv = s_gm[idx % 2].n
            if idx + 1 < len(gm_srcs):
                if idx >= 1:
                    s_.wait(s_s)
                gm_fill(idx + 1)
            wi, j = gm_srcs[idx]
            s_.wait(s_gm[idx % 2], wv)
            s_.activation(out=gm_slab[idx % 2][:, :], in_=gm_slab[idx % 2][:, :],
                          func=AF.Abs,
                          accum_out=sb_gm2[:, wi * 44 + j:wi * 44 + j + 1],
                          inc=inc1(s_s))

        mark('latents')
        lt_marks = []
        lte_done = [0, 0]
        u = 0
        for grp in range(4):
            wsrc = wqd if grp < 3 else wkvd
            wcol0 = grp * 512 if grp < 3 else 0
            for kc in range(NCH):
                buf = u % 2
                for eng in (q_, g_):
                    if u >= 2:
                        eng.wait(s_mm, lt_marks[u - 2])
                    else:
                        eng.wait(s_mm)
                        eng.wait(s_v)
                q_.dma_start(out=wo_stg[buf][:, :],
                             in_=wsrc[ts(kc, 128), wcol0:wcol0 + 512],
                             inc=dinc(s_lt[buf]))
                g_.dma_start(out=wo_hi[buf][:, :],
                             in_=wsrc[ts(kc, 128), wcol0:wcol0 + 512],
                             inc=dinc(s_ltb[buf]))
                v_.wait(s_lt[buf])
                v_.wait(s_ltb[buf])
                v_.tensor_tensor(wo_lo[buf][:, :], wo_stg[buf][:, :],
                                 wo_hi[buf][:, :], OP.subtract, inc=inc1(s_v))
                t_.wait(s_v)
                st = (kc == 0); sp = (kc == NCH - 1)
                for mc in range(4):
                    wht = wo_hi[buf][:, ts(mc, 128)]; wlt = wo_lo[buf][:, ts(mc, 128)]
                    for tb in range(2):
                        xh = xn_piece(True, kc, ts(tb, 512))
                        xl = xn_piece(False, kc, ts(tb, 512))
                        po = pbank[mc * 2 + tb]
                        t_.matmul(po[:, :], wht, xh, start=st, stop=False)
                        t_.matmul(po[:, :], wht, xl, start=False, stop=False)
                    for tb in range(2):
                        xh = xn_piece(True, kc, ts(tb, 512))
                        po = pbank[mc * 2 + tb]
                        t_.matmul(po[:, :], wlt, xh, start=False, stop=sp,
                                  inc=inc1(s_mm) if (mc == 3 and tb == 1) else None)
                lt_marks.append(s_mm.n)
                u += 1
            v_.wait(s_mm)
            for mc in range(4):
                eb = mc % 2
                if lte_done[eb]:
                    v_.wait(s_lte[eb], lte_done[eb])
                for tb in range(2):
                    v_.tensor_copy(out=lt_st[eb][:, ts(tb, 512)],
                                   in_=pbank[mc * 2 + tb][:, :],
                                   inc=inc1(s_v) if tb == 1 else None)
                q_.wait(s_v)
                row0 = grp * 512 + mc * 128
                q_.dma_start(out=lat_in[row0:row0 + 128, :], in_=lt_st[eb][:, :],
                             inc=dinc(s_lte[eb]))
                lte_done[eb] = s_lte[eb].n
        g_.wait(s_lte[0], lte_done[0])
        g_.wait(s_lte[1], lte_done[1])
        g_.collective_compute(
            "AllGather", OP.bypass, replica_groups=RG,
            ins=[lat_in[:, :]], outs=[lat_out[:, :]], inc=inc1(s_cc))
        ag1_done = s_cc.n

        mark('gamma')
        NW3 = 5632 * 2048
        for wi in range(3):
            v_.wait(s_s)
            v_.drain()
            v_.reduce_sum(sb_gm[:, wi:wi + 1], sb_gm2[:, wi * 44:(wi + 1) * 44],
                          axis=mybir.AxisListType.X, inc=inc1(s_v))
            t_.wait(s_v)
            t_.matmul(pbank[7][0:1, 0:1], sb_onescol[:, :], sb_gm[:, wi:wi + 1],
                      start=True, stop=True, inc=inc1(s_mm))
            v_.wait(s_mm)
            gv = sb_sm[0:1, 4:5]
            gvn = sb_sm[0:1, 5:6]
            v_.tensor_scalar(gv, pbank[7][0:1, 0:1], 0.5 / NW3, 0.5e-8,
                             OP.mult, OP.add)
            v_.drain()
            v_.tensor_scalar_mul(gvn, gv, -1.0, inc=inc1(s_v))
            t_.wait(s_v)
            t_.matmul(pbank[7][:, 0:1], sb_ones[:, :], gv, start=True, stop=True)
            t_.matmul(pbank[7][:, 1:2], sb_ones[:, :], gvn,
                      start=True, stop=True, inc=inc1(s_mm))
            v_.wait(s_mm)
            v_.tensor_copy(out=sb_thr[:, 2 * wi:2 * wi + 1], in_=pbank[7][:, 0:1])
            v_.tensor_copy(out=sb_thr[:, 2 * wi + 1:2 * wi + 2],
                           in_=pbank[7][:, 1:2], inc=inc1(s_v))

        mark('x1x2')
        v_.wait(s_lte[0], lte_done[0])
        v_.wait(s_lte[1], lte_done[1])
        x1_marks = []
        x1_vmarks = []
        ep1_marks = []
        hid_done = [0, 0]
        NU1 = 44

        def x1_fill(u):
            row = u * 128
            if u == 0:
                q_.wait(s_mm)
                q_.wait(s_v)
                g_.wait(s_mm)
                g_.wait(s_v)
            else:
                q_.wait(s_v, x1_vmarks[u - 1])
                g_.wait(s_v, x1_vmarks[u - 1])
            q_.dma_start(out=xw1_stg[:, :], in_=w1p[row:row + 128, :],
                         inc=dinc(s_x1w[0]))
            g_.dma_start(out=xw2_stg[:, :], in_=w2p[row:row + 128, :],
                         inc=dinc(s_x1w[1]))

        def x1_quant(u):
            buf = u % 2
            if u >= 2:
                v_.wait(s_mm, x1_marks[u - 2])
            v_.wait(s_x1w[0])
            v_.drain()
            v_.tensor_single_scalar(xw_tmp[:, :], xw1_stg[:, :],
                                    sb_thr[:, 1:2], OP.is_lt)
            v_.drain()
            v_.scalar_tensor_tensor(q1s[buf][:, :], xw1_stg[:, :],
                                    sb_thr[:, 0:1], xw_tmp[:, :],
                                    OP.is_gt, OP.subtract)
            v_.wait(s_x1w[1])
            v_.drain()
            v_.tensor_single_scalar(xw_tmp2[:, :], xw2_stg[:, :],
                                    sb_thr[:, 3:4], OP.is_lt)
            v_.drain()
            v_.scalar_tensor_tensor(q2s[buf][:, :], xw2_stg[:, :],
                                    sb_thr[:, 2:3], xw_tmp2[:, :],
                                    OP.is_gt, OP.subtract, inc=inc1(s_v))
            x1_vmarks.append(s_v.n)

        x1_fill(0)
        x1_quant(0)
        for u in range(NU1):
            mg, kh = u // 2, u % 2
            buf = u % 2
            if u + 1 < NU1:
                x1_fill(u + 1)
                x1_quant(u + 1)
            if kh == 0 and mg >= 1:
                t_.wait(s_v, ep1_marks[mg - 1])
            else:
                t_.wait(s_v, x1_vmarks[u])
            for kc8 in range(8):
                kc = kh * 8 + kc8
                st = (kc == 0); sp = (kc == NCH - 1)
                for mc in range(2):
                    w1t = q1s[buf][:, kc8 * 256 + mc * 128:
                                   kc8 * 256 + (mc + 1) * 128]
                    w2t = q2s[buf][:, kc8 * 256 + mc * 128:
                                   kc8 * 256 + (mc + 1) * 128]
                    for tb in range(2):
                        xh = xn_piece(True, kc, ts(tb, 512))
                        xl = xn_piece(False, kc, ts(tb, 512))
                        p1 = pbank[mc * 2 + tb]
                        p2 = pbank[4 + mc * 2 + tb]
                        t_.matmul(p1[:, :], w1t, xh, start=st, stop=False)
                        t_.matmul(p1[:, :], w1t, xl, start=False, stop=sp)
                        t_.matmul(p2[:, :], w2t, xh, start=st, stop=False)
                        t_.matmul(p2[:, :], w2t, xl, start=False, stop=sp,
                                  inc=inc1(s_mm)
                                  if (kc8 == 7 and mc == 1 and tb == 1) else None)
            x1_marks.append(s_mm.n)
            if kh == 1:
                s_.wait(s_mm, x1_marks[u])
                for mc in range(2):
                    mabs = mg * 2 + mc
                    hb = mabs % 2
                    s_.wait(s_v)
                    for tb in range(2):
                        s_.activation(out=hd_s[:, ts(tb, 512)],
                                      in_=pbank[mc * 2 + tb][:, :],
                                      func=AF.Sigmoid,
                                      inc=inc1(s_s) if tb == 1 else None)
                    v_.wait(s_s)
                    for tb in range(2):
                        v_.tensor_tensor(hd_f[:, ts(tb, 512)], hd_s[:, ts(tb, 512)],
                                         pbank[4 + mc * 2 + tb][:, :], OP.mult)
                    if hid_done[hb]:
                        v_.wait(s_hid[hb], hid_done[hb])
                    v_.drain()
                    v_.tensor_copy(out=hd_h[hb][:, :], in_=hd_f[:, :])
                    v_.drain()
                    v_.tensor_tensor(hd_l[hb][:, :], hd_f[:, :], hd_h[hb][:, :],
                                     OP.subtract, inc=inc1(s_v))
                    q_.wait(s_v)
                    q_.dma_start(out=hid_hi[ts(mabs, 128), :], in_=hd_h[hb][:, :],
                                 inc=dinc(s_hid[hb]))
                    q_.dma_start(out=hid_lo[ts(mabs, 128), :], in_=hd_l[hb][:, :],
                                 inc=dinc(s_hid[hb]))
                    hid_done[hb] = s_hid[hb].n
                ep1_marks.append(s_v.n)

        mark('kvlat')
        g_.wait(s_cc, ag1_done)
        g_.wait(s_v)
        g_.wait(s_mm)
        g_.wait(s_hid[0], hid_done[0])
        g_.wait(s_hid[1], hid_done[1])
        q_.wait(s_cc, ag1_done)
        q_.wait(s_v)
        q_.wait(s_mm)
        q_.wait(s_hid[0], hid_done[0])
        q_.wait(s_hid[1], hid_done[1])
        for kc in range(4):
            for sl in range(2):
                row0 = sl * 2048 + 1536 + kc * 128
                q_.dma_start(out=kvstg[:, ts(sl, 1024)],
                             in_=lat_out[row0:row0 + 128, :], inc=dinc(s_kvla))
                g_.dma_start(out=kvl_hi[:, kc * 2048 + sl * 1024:
                                        kc * 2048 + (sl + 1) * 1024],
                             in_=lat_out[row0:row0 + 128, :], inc=dinc(s_kvlb))
            v_.wait(s_kvla)
            v_.wait(s_kvlb)
            v_.tensor_tensor(kvl_lo[:, ts(kc, 2048)], kvstg[:, :],
                             kvl_hi[:, ts(kc, 2048)], OP.subtract, inc=inc1(s_v))
            if kc < 3:
                q_.wait(s_v)
                g_.wait(s_v)

        def split_psum(ps_ap, hi_ap, lo_ap):
            s_.copy(out=hi_ap, in_=ps_ap, inc=inc1(s_s))
            v_.wait(s_s)
            v_.scalar_tensor_tensor(lo_ap, ps_ap, 0.0, hi_ap,
                                    OP.add, OP.subtract, inc=inc1(s_v))

        def rope_split(ps_ap, col, hi_ap, lo_ap):
            cs = sb_cos[:, col:col + 512]
            sn = sb_sin[:, col:col + 512]
            v_.drain()
            v_.tensor_tensor(rsc1[:, :], ps_ap, cs, OP.mult)
            v_.stream_shuffle(out=rsc2[:, :], in_=ps_ap,
                              mask=[i ^ 1 for i in range(32)])
            v_.drain()
            v_.tensor_tensor(rsc3[:, :], rsc2[:, :], sn, OP.mult)
            v_.drain()
            v_.tensor_tensor(rsc2[:, :], rsc1[:, :], rsc3[:, :], OP.add,
                             inc=inc1(s_v))
            s_.wait(s_v)
            s_.copy(out=hi_ap, in_=rsc2[:, :], inc=inc1(s_s))
            v_.wait(s_s)
            v_.tensor_tensor(lo_ap, rsc2[:, :], hi_ap, OP.subtract,
                             inc=inc1(s_v))

        mark('attn')
        qk_copy_marks = []
        qk_exp_marks = []
        av_done = 0
        for lh in range(8):
            q_.wait(s_mm)
            g_.wait(s_mm)
            if lh == 0:
                q_.wait(s_v)
                g_.wait(s_v)
            for kh in range(2):
                row = (lh * 2 + kh) * 128
                if kh == 1:
                    q_.wait(s_v)
                q_.dma_start(out=qw_stg[:, :], in_=wqwp[row:row + 128, :],
                             inc=dinc(s_qwsa[kh]))
                g_.dma_start(out=qw_hi[kh][:, :], in_=wqwp[row:row + 128, :],
                             inc=dinc(s_qwsb[kh]))
                v_.wait(s_qwsa[kh])
                v_.wait(s_qwsb[kh])
                v_.drain()
                v_.tensor_tensor(qw_lo[kh][:, :], qw_stg[:, :],
                                 qw_hi[kh][:, :], OP.subtract, inc=inc1(s_v))
            qe_marks = []
            for th in range(2):
                for kc in range(12):
                    i = th * 12 + kc
                    buf = kc % 2
                    if i >= 2:
                        q_.wait(s_mm, qe_marks[i - 2])
                        g_.wait(s_mm, qe_marks[i - 2])
                    else:
                        q_.wait(s_mm)
                        g_.wait(s_mm)
                    row0 = th * 2048 + kc * 128
                    q_.dma_start(out=ql_stg[buf][:, :],
                                 in_=lat_out[row0:row0 + 128, :],
                                 inc=dinc(s_qla2[buf]))
                    g_.dma_start(out=ql_hi[buf][:, :],
                                 in_=lat_out[row0:row0 + 128, :],
                                 inc=dinc(s_qlb2[buf]))
                    v_.wait(s_qla2[buf])
                    v_.wait(s_qlb2[buf])
                    v_.drain()
                    v_.tensor_tensor(ql_lo[buf][:, :], ql_stg[buf][:, :],
                                     ql_hi[buf][:, :], OP.subtract, inc=inc1(s_v))
                    t_.wait(s_v)
                    st = (kc == 0); sp = (kc == 11)
                    kh = kc // 6
                    off = (kc % 6) * 192
                    wnh = qw_hi[kh][:, off:off + 128]
                    wnl = qw_lo[kh][:, off:off + 128]
                    wrh = qw_hi[kh][:, off + 128:off + 192]
                    wrl = qw_lo[kh][:, off + 128:off + 192]
                    for tb in range(2):
                        qlh = ql_hi[buf][:, ts(tb, 512)]
                        qll = ql_lo[buf][:, ts(tb, 512)]
                        pn = pbank[tb]
                        pr = pbank[2 + tb][0:64, :]
                        t_.matmul(pn[:, :], wnh, qlh, start=st, stop=False)
                        t_.matmul(pn[:, :], wnh, qll, start=False, stop=False)
                        t_.matmul(pr, wrh, qlh, start=st, stop=False)
                        t_.matmul(pr, wrh, qll, start=False, stop=False)
                    for tb in range(2):
                        qlh = ql_hi[buf][:, ts(tb, 512)]
                        pn = pbank[tb]
                        pr = pbank[2 + tb][0:64, :]
                        t_.matmul(pn[:, :], wnl, qlh, start=False, stop=sp)
                        t_.matmul(pr, wrl, qlh, start=False, stop=sp,
                                  inc=inc1(s_mm) if tb == 1 else None)
                    qe_marks.append(s_mm.n)
                s_.wait(s_mm)
                v_.wait(s_mm)
                if av_done:
                    v_.wait(s_av, av_done)
                for tb in range(2):
                    col = th * 1024 + tb * 512
                    split_psum(pbank[tb][:, :], qn_hi[:, col:col + 512],
                               qn_lo[:, col:col + 512])
                    rope_split(pbank[2 + tb][0:64, :], col,
                               qr_hi[:, col:col + 512], qr_lo[:, col:col + 512])

            q_.wait(s_mm)
            g_.wait(s_mm)
            q_.dma_start(out=kw_stg[:, :], in_=wkwp[ts(lh, 128), :],
                         inc=dinc(s_kw))
            g_.dma_start(out=kw_hi[:, :], in_=wkwp[ts(lh, 128), :],
                         inc=dinc(s_kwb))
            v_.wait(s_kw)
            v_.wait(s_kwb)
            v_.tensor_tensor(kw_lo[:, :], kw_stg[:, :], kw_hi[:, :], OP.subtract,
                             inc=inc1(s_v))
            t_.wait(s_v)
            for kc in range(4):
                st = (kc == 0); sp = (kc == 3)
                wnh = kw_hi[:, kc * 128:(kc + 1) * 128]
                wnl = kw_lo[:, kc * 128:(kc + 1) * 128]
                wrh = kw_hi[:, 512 + kc * 64:512 + (kc + 1) * 64]
                wrl = kw_lo[:, 512 + kc * 64:512 + (kc + 1) * 64]
                for tb in range(4):
                    kvh = kvl_hi[:, kc * 2048 + tb * 512: kc * 2048 + (tb + 1) * 512]
                    kvl_ = kvl_lo[:, kc * 2048 + tb * 512: kc * 2048 + (tb + 1) * 512]
                    t_.matmul(pbank[tb][:, :], wnh, kvh, start=st, stop=False)
                    t_.matmul(pbank[tb][:, :], wnh, kvl_, start=False, stop=False)
                    t_.matmul(pbank[4 + tb][0:64, :], wrh, kvh, start=st, stop=False)
                    t_.matmul(pbank[4 + tb][0:64, :], wrh, kvl_, start=False, stop=False)
                for tb in range(4):
                    kvh = kvl_hi[:, kc * 2048 + tb * 512: kc * 2048 + (tb + 1) * 512]
                    t_.matmul(pbank[tb][:, :], wnl, kvh, start=False, stop=sp)
                    t_.matmul(pbank[4 + tb][0:64, :], wrl, kvh, start=False, stop=sp,
                              inc=inc1(s_mm) if (sp and tb == 3) else None)
            s_.wait(s_mm)
            v_.wait(s_mm)
            if av_done:
                v_.wait(s_av, av_done)
            for tb in range(2):
                col = tb * 512
                split_psum(pbank[tb][:, :], kn_hi[:, col:col + 512],
                           kn_lo[:, col:col + 512])
            q_.wait(s_mm)
            g_.wait(s_mm)
            q_.dma_start(out=vw_stg[:, :], in_=wvwp[ts(lh, 128), :],
                         inc=dinc(s_vw))
            g_.dma_start(out=vw_hi[:, :], in_=wvwp[ts(lh, 128), :],
                         inc=dinc(s_vwb))
            v_.wait(s_vw)
            v_.wait(s_vwb)
            v_.drain()
            v_.tensor_tensor(vw_lo[:, :], vw_stg[:, :], vw_hi[:, :], OP.subtract,
                             inc=inc1(s_v))
            vwlo_mark = s_v.n
            for tb in range(2, 4):
                col = tb * 512
                split_psum(pbank[tb][:, :], kn_hi[:, col:col + 512],
                           kn_lo[:, col:col + 512])
            for tb in range(4):
                col = tb * 512
                rope_split(pbank[4 + tb][0:64, :], col,
                           kr_hi[:, col:col + 512], kr_lo[:, col:col + 512])
            t_.wait(s_v, vwlo_mark)
            vev_marks = []
            for tc in range(16):
                pb = pbank[tc % 2]
                if tc >= 2:
                    t_.wait(s_v, vev_marks[tc - 2])
                for kc in range(4):
                    lvh = kvl_hi[:, kc * 2048 + tc * 128: kc * 2048 + tc * 128 + 128]
                    lvl = kvl_lo[:, kc * 2048 + tc * 128: kc * 2048 + tc * 128 + 128]
                    wh = vw_hi[:, kc * 192:(kc + 1) * 192]
                    wl = vw_lo[:, kc * 192:(kc + 1) * 192]
                    t_.matmul(pb[:, 0:192], lvh, wh, start=(kc == 0), stop=False)
                    t_.matmul(pb[:, 0:192], lvh, wl, start=False, stop=False)
                    t_.matmul(pb[:, 0:192], lvl, wh, start=False, stop=(kc == 3),
                              inc=inc1(s_mm) if kc == 3 else None)
                s_.wait(s_mm)
                v_.wait(s_mm)
                split_psum(pb[:, 0:192], av_hi[:, tc * 192:(tc + 1) * 192],
                           av_lo[:, tc * 192:(tc + 1) * 192])
                vev_marks.append(s_v.n)

            for qp in range(8):
                span = (qp + 1) * 256
                nkb = span // 256
                t_.wait(s_v)
                for q2 in range(2):
                    qt = 2 * qp + q2
                    qsl = slice(qt * 128, (qt + 1) * 128)
                    if q2 == 1:
                        t_.wait(s_v, qk_copy_marks[-1])
                    passes = [
                        (qn_hi[:, qsl], kn_hi, True, False),
                        (qn_hi[:, qsl], kn_lo, False, False),
                        (qn_lo[:, qsl], kn_hi, False, False),
                        (qr_hi[:, qsl], kr_hi, False, False),
                        (qr_hi[:, qsl], kr_lo, False, False),
                        (qr_lo[:, qsl], kr_hi, False, True),
                    ]
                    for (lt, rt, isfirst, islast) in passes:
                        for kb in range(nkb):
                            stf = isfirst and (kb % 2 == 0)
                            spf = islast and (kb % 2 == 1 or kb == nkb - 1)
                            t_.matmul(psA[:, kb * 256:(kb + 1) * 256],
                                      lt, rt[:, kb * 256:(kb + 1) * 256],
                                      start=stf, stop=spf,
                                      inc=inc1(s_mm)
                                      if (islast and kb == nkb - 1) else None)
                    v_.wait(s_mm)
                    if qk_exp_marks:
                        v_.wait(s_s, qk_exp_marks[-1])
                    v_.drain()
                    v_.tensor_copy(out=p_f[:, 0:span], in_=psA[:, 0:span],
                                   inc=inc1(s_v))
                    qk_copy_marks.append(s_v.n)
                    v_.drain()
                    mask = sb_maske if qt % 2 == 0 else sb_masko
                    v_.tensor_tensor(p_f[:, span - 256:span],
                                     p_f[:, span - 256:span], mask[:, :], OP.add)
                    v_.drain()
                    v_.reduce_max(sb_sm[:, 0:1], p_f[:, 0:span],
                                  axis=mybir.AxisListType.X, negate=True)
                    v_.drain()
                    v_.tensor_scalar_mul(sb_sm[:, 1:2], sb_sm[:, 0:1], CATT,
                                         inc=inc1(s_v))
                    s_.wait(s_v)
                    s_.activation(out=p_f[:, 0:span], in_=p_f[:, 0:span],
                                  func=AF.Exp, scale=CATT, bias=sb_sm[:, 1:2],
                                  accum_out=sb_sm[:, 2:3], inc=inc1(s_s))
                    qk_exp_marks.append(s_s.n)
                    v_.wait(s_s)
                    v_.reciprocal(sb_sm[:, 3:4], sb_sm[:, 2:3])
                    v_.drain()
                    v_.tensor_scalar_mul(ph_p[:, q2 * span:q2 * span + span],
                                         p_f[:, 0:span], sb_sm[:, 3:4])
                    v_.drain()
                    v_.scalar_tensor_tensor(pl_p[:, q2 * span:q2 * span + span],
                                            p_f[:, 0:span], sb_sm[:, 3:4],
                                            ph_p[:, q2 * span:q2 * span + span],
                                            OP.mult, OP.subtract, inc=inc1(s_v))
                t_.wait(s_v)
                nkc = span // 128
                tr_marks = []
                ev_marks = []
                pv_marks = []

                def pt_sl(set_, pi, q2):
                    base = pi * 256 + q2 * 128
                    t_ = pt_ps if set_ == 0 else pt_ps2
                    return t_[:, base:base + 128]

                def emit_transp(kc):
                    set_ = kc % 2
                    if kc >= 2:
                        t_.wait(s_v, ev_marks[kc - 2])
                    for (pi, ptile) in enumerate((ph_p, pl_p)):
                        for q2 in range(2):
                            t_.transpose(pt_sl(set_, pi, q2),
                                         ptile[:, q2 * span + kc * 128:
                                               q2 * span + (kc + 1) * 128],
                                         sb_ident[:, :],
                                         inc=inc1(s_mm)
                                         if (pi == 1 and q2 == 1) else None)
                    tr_marks.append(s_mm.n)

                def emit_evac(kc):
                    set_ = kc % 2
                    v_.wait(s_mm, tr_marks[kc])
                    if kc >= 2:
                        v_.wait(s_mm, pv_marks[kc - 2])
                    src_t = pt_ps if set_ == 0 else pt_ps2
                    v_.tensor_copy(out=ptr_hi[:, ts(set_, 256)],
                                   in_=src_t[:, 0:256])
                    v_.tensor_copy(out=ptr_lo[:, ts(set_, 256)],
                                   in_=src_t[:, 256:512],
                                   inc=inc1(s_v))
                    ev_marks.append(s_v.n)

                emit_transp(0)
                emit_evac(0)
                for kc in range(nkc):
                    set_ = kc % 2
                    if kc + 1 < nkc:
                        emit_transp(kc + 1)
                    t_.wait(s_v, ev_marks[kc])
                    st = (kc == 0); sp = (kc == nkc - 1)
                    vh0 = av_hi[:, kc * 192:kc * 192 + 128]
                    vl0 = av_lo[:, kc * 192:kc * 192 + 128]
                    vh1 = av_hi[:, kc * 192 + 128:kc * 192 + 192]
                    vl1 = av_lo[:, kc * 192 + 128:kc * 192 + 192]
                    pa = pbank[4][:, 0:256]
                    pb_ = pbank[5][0:64, 0:256]
                    ph_s = ptr_hi[:, ts(set_, 256)]
                    pl_s = ptr_lo[:, ts(set_, 256)]
                    t_.matmul(pa, vh0, ph_s, start=st, stop=False)
                    t_.matmul(pa, vh0, pl_s, start=False, stop=False)
                    t_.matmul(pa, vl0, ph_s, start=False, stop=sp)
                    t_.matmul(pb_, vh1, ph_s, start=st, stop=False)
                    t_.matmul(pb_, vh1, pl_s, start=False, stop=False)
                    t_.matmul(pb_, vl1, ph_s, start=False, stop=sp,
                              inc=inc1(s_mm))
                    pv_marks.append(s_mm.n)
                    if kc + 1 < nkc:
                        emit_evac(kc + 1)
                v_.wait(s_mm)
                if av_done:
                    v_.wait(s_av, av_done)
                v_.tensor_copy(out=pv_st[:, 0:256], in_=pbank[4][:, 0:256])
                v_.tensor_copy(out=pv_st[0:64, 256:512], in_=pbank[5][0:64, 0:256],
                               inc=inc1(s_v))
                q_.wait(s_v)
                r0 = lh * 192
                q_.dma_start(out=aout1[r0:r0 + 128, ts(qp, 256)],
                             in_=pv_st[:, 0:256], inc=dinc(s_av))
                q_.dma_start(out=aout1[r0 + 128:r0 + 192, ts(qp, 256)],
                             in_=pv_st[0:64, 256:512], inc=dinc(s_av))
                av_done = s_av.n

        g_.wait(s_av, av_done)
        g_.collective_compute(
            "AllGather", OP.bypass, replica_groups=RG,
            ins=[aout1[:, :]], outs=[aout1_out[:, :]], inc=inc1(s_cc))
        ag2_done = s_cc.n

        mark('down1')
        dn1_marks = []
        qv_marks = []
        u = 0
        for tb in range(2):
            for mcg in range(2):
                t_.wait(s_v)
                for kcq in range(11):
                    buf = u % 2
                    for sf, (eng, sem) in enumerate(((g_, s_w3s[0]),
                                                     (q_, s_w3s[1]))):
                        if u >= 1:
                            eng.wait(s_v, qv_marks[u - 1])
                        else:
                            eng.wait(s_mm)
                            eng.wait(s_v)
                            eng.wait(s_hid[0], hid_done[0])
                            eng.wait(s_hid[1], hid_done[1])
                        r0 = kcq * 512 + sf * 256
                        eng.dma_start(
                            out=sb3(dn3_stg[sf][:, :], 1024),
                            in_=w3p[r0:r0 + 256, ts(mcg, 1024)]
                                .rearrange("(c p) m -> p c m", p=128),
                            inc=dinc(sem))
                    if u >= 2:
                        g_.wait(s_mm, dn1_marks[u - 2])
                        q_.wait(s_mm, dn1_marks[u - 2])
                    for kc4 in range(4):
                        kc = kcq * 4 + kc4
                        g_.dma_start(out=dn_fh[buf][:, ts(kc4, 512)],
                                     in_=hid_hi[ts(kc, 128), ts(tb, 512)],
                                     inc=dinc(s_dn[buf]))
                        q_.dma_start(out=dn_fl[buf][:, ts(kc4, 512)],
                                     in_=hid_lo[ts(kc, 128), ts(tb, 512)],
                                     inc=dinc(s_dn1[buf]))
                    if u >= 2:
                        v_.wait(s_mm, dn1_marks[u - 2])
                    elif u == 0:
                        v_.wait(s_mm)
                    for sf in range(2):
                        v_.wait(s_w3s[sf])
                        v_.drain()
                        v_.tensor_single_scalar(w3_tmp2[:, :], dn3_stg[sf][:, :],
                                                sb_thr[:, 5:6], OP.is_lt)
                        v_.drain()
                        v_.scalar_tensor_tensor(
                            dn_w[buf][:, sf * 2048:(sf + 1) * 2048],
                            dn3_stg[sf][:, :], sb_thr[:, 4:5], w3_tmp2[:, :],
                            OP.is_gt, OP.subtract,
                            inc=inc1(s_v) if sf == 1 else None)
                    qv_marks.append(s_v.n)
                    t_.wait(s_v, qv_marks[u])
                    t_.wait(s_dn[buf])
                    t_.wait(s_dn1[buf])
                    for kc4 in range(4):
                        st = (kcq == 0 and kc4 == 0); sp = (kcq == 10 and kc4 == 3)
                        for mc in range(8):
                            wt = dn_w[buf][:, kc4 * 1024 + mc * 128:
                                           kc4 * 1024 + (mc + 1) * 128]
                            po = pbank[mc]
                            t_.matmul(po[:, :], wt, dn_fh[buf][:, ts(kc4, 512)],
                                      start=st, stop=False)
                            t_.matmul(po[:, :], wt, dn_fl[buf][:, ts(kc4, 512)],
                                      start=False, stop=sp,
                                      inc=inc1(s_mm)
                                      if (kc4 == 3 and mc == 7) else None)
                    dn1_marks.append(s_mm.n)
                    u += 1
                v_.wait(s_mm)
                for mc in range(8):
                    mabs = mcg * 8 + mc
                    xd = x_res[:, mabs * 1024 + tb * 512: mabs * 1024 + (tb + 1) * 512]
                    v_.tensor_tensor(xd, pbank[mc][:, :], xd, OP.add,
                                     inc=inc1(s_v) if mc == 7 else None)

        mark('wo1')
        g_.wait(s_cc, ag2_done)
        q_.wait(s_cc, ag2_done)
        wo1_marks = []
        w1r_done = [0, 0]
        u = 0
        for mcg in range(4):
            t_.wait(s_v)
            for kc in range(24):
                buf = u % 2
                for eng in (q_, g_):
                    if u >= 2:
                        eng.wait(s_mm, wo1_marks[u - 2])
                    else:
                        eng.wait(s_mm)
                        eng.wait(s_v)
                q_.dma_start(out=wo_stg[buf][:, :],
                             in_=wo1[ts(kc, 128), ts(mcg, 512)], inc=dinc(s_lt[buf]))
                g_.dma_start(out=wo_hi[buf][:, :],
                             in_=wo1[ts(kc, 128), ts(mcg, 512)], inc=dinc(s_ltb[buf]))

                def mk_dyn(dst, krow, sem):
                    def op(eng):
                        inst = eng.dma_start(
                            out=dst,
                            in_=aout1_out[krow:krow + 128,
                                          bass.ds(tok_cell["v"], 1024)])
                        inst.then_inc(sem.h, 16)
                    return op
                s_w1r[buf].n += 16
                q_.ops.append(mk_dyn(w1r_stg[buf][:, :], kc * 128, s_w1r[buf]))
                v_.wait(s_lt[buf])
                v_.wait(s_ltb[buf])
                v_.tensor_tensor(wo_lo[buf][:, :], wo_stg[buf][:, :],
                                 wo_hi[buf][:, :], OP.subtract)
                v_.wait(s_w1r[buf])
                v_.tensor_copy(out=w1r_hi[buf][:, :], in_=w1r_stg[buf][:, :])
                v_.drain()
                v_.tensor_tensor(w1r_lo[buf][:, :], w1r_stg[buf][:, :],
                                 w1r_hi[buf][:, :], OP.subtract, inc=inc1(s_v))
                t_.wait(s_v)
                st = (kc == 0); sp = (kc == 23)
                for mc in range(4):
                    wht = wo_hi[buf][:, ts(mc, 128)]
                    wlt = wo_lo[buf][:, ts(mc, 128)]
                    for tb in range(2):
                        rh = w1r_hi[buf][:, ts(tb, 512)]
                        rl = w1r_lo[buf][:, ts(tb, 512)]
                        po = pbank[mc * 2 + tb]
                        t_.matmul(po[:, :], wht, rh, start=st, stop=False)
                        t_.matmul(po[:, :], wht, rl, start=False, stop=False)
                    for tb in range(2):
                        rh = w1r_hi[buf][:, ts(tb, 512)]
                        po = pbank[mc * 2 + tb]
                        t_.matmul(po[:, :], wlt, rh, start=False, stop=sp,
                                  inc=inc1(s_mm) if (mc == 3 and tb == 1) else None)
                wo1_marks.append(s_mm.n)
                u += 1
            v_.wait(s_mm)
            for mc in range(4):
                mabs = mcg * 4 + mc
                for tb in range(2):
                    xd = x_res[:, mabs * 1024 + tb * 512: mabs * 1024 + (tb + 1) * 512]
                    v_.tensor_tensor(xd, pbank[mc * 2 + tb][:, :], xd, OP.add,
                                     inc=inc1(s_v) if (mc == 3 and tb == 1) else None)

    for name in dbg:
        if name == "xn":
            dh = dout("dbg_xnh", [128, 16384], BF16)
            dl = dout("dbg_xnl", [128, 16384], BF16)
            g_.wait(s_v)
            g_.dma_start(out=dh[:, :], in_=xn_hi[:, :], inc=dinc(s_dbg))
            g_.dma_start(out=dl[:, :], in_=xn_lo[:, :], inc=dinc(s_dbg))
        if name == "kv":
            d = dout("dbg_kv", [64, 2080])
            g_.wait(s_v)
            g_.dma_start(out=d[:, :], in_=kv_sb[:, :], inc=dinc(s_dbg))
        if name == "aout0":
            dh = dout("dbg_aout0h", [2048, 1024], BF16)
            dl = dout("dbg_aout0l", [2048, 1024], BF16)
            g_.wait(s_ao, aout_done)
            g_.dma_start(out=dh[:, :], in_=aout0_hi[:, :], inc=dinc(s_dbg))
            g_.dma_start(out=dl[:, :], in_=aout0_lo[:, :], inc=dinc(s_dbg))
        if name == "fuse":
            dh = dout("dbg_fuseh", [8192, 1024], BF16)
            g_.wait(s_fu[0], fuse_done[0])
            g_.wait(s_fu[1], fuse_done[1])
            g_.dma_start(out=dh[:, :], in_=fuse_hi[:, :], inc=dinc(s_dbg))

    q_.wait(s_v)
    q_.dma_start(out=chunked(y), in_=sb3(x_res[:, :], 1024), inc=dinc(s_st))
    q_.wait(s_st)
    g_.wait(s_dbg)

    p.finalize()
    return nc



def make_tables():
    ident = np.eye(128, dtype=np.float32)
    ones = np.ones((1, 128), dtype=np.float32)
    DR = 64
    inv = 1.0 / (10000.0 ** (np.arange(0, DR, 2, dtype=np.float32) / DR))
    t = np.arange(2048, dtype=np.float32)
    fr = np.outer(t, inv)
    cos = np.cos(fr).astype(np.float32)
    sin = np.sin(fr).astype(np.float32)
    cosd = np.zeros((64, 2048), np.float32)
    sind = np.zeros((64, 2048), np.float32)
    cosd[0::2] = cos.T; cosd[1::2] = cos.T
    sind[0::2] = -sin.T; sind[1::2] = sin.T
    r = np.arange(128)[:, None]; c = np.arange(256)[None, :]
    mask_e = np.where(c <= r, 0.0, NEG_BIG).astype(np.float32)
    mask_o = np.where((c - 128) <= r, 0.0, NEG_BIG).astype(np.float32)
    return ident, ones, cosd, sind, mask_e, mask_o


def make_in_maps(inp):
    import ml_dtypes
    BF = ml_dtypes.bfloat16
    ident, ones, cosd, sind, mask_e, mask_o = make_tables()
    x = np.asarray(inp["L0_attn_hidden_states"])
    in_maps = []
    for c in range(8):
        b, hf = c // 2, c % 2
        wq = np.asarray(inp["L0_attn_Wq"]); wk = np.asarray(inp["L0_attn_Wk"])
        wv = np.asarray(inp["L0_attn_Wv"])
        wg8 = np.asarray(inp["L0_ffn_w_gate"]); wu8 = np.asarray(inp["L0_ffn_w_up"])
        wd8 = np.asarray(inp["L0_ffn_w_down"])
        w1 = np.asarray(inp["L1_ffn_w1"]); w2 = np.asarray(inp["L1_ffn_w2"])
        w3 = np.asarray(inp["L1_ffn_w3"])
        wqn = np.asarray(inp["L1_attn_wq_nope"]).reshape(1536, 16, 128)[:, hf * 8:(hf + 1) * 8].reshape(1536, 1024)
        wqr = np.asarray(inp["L1_attn_wq_rope"]).reshape(1536, 16, 64)[:, hf * 8:(hf + 1) * 8].reshape(1536, 512)
        wkn = np.asarray(inp["L1_attn_wk_nope"]).reshape(512, 16, 128)[:, hf * 8:(hf + 1) * 8].reshape(512, 1024)
        wkr = np.asarray(inp["L1_attn_wk_rope"]).reshape(512, 16, 64)[:, hf * 8:(hf + 1) * 8].reshape(512, 512)
        wvu = np.asarray(inp["L1_attn_wv_up"]).reshape(512, 16, 192)[:, hf * 8:(hf + 1) * 8].reshape(512, 1536)
        wqp = wq.reshape(16, 128, 32, 64).transpose(2, 1, 0, 3).reshape(4096, 1024)
        wkp = wk.reshape(16, 128, 8, 256).transpose(2, 1, 0, 3).reshape(1024, 4096)
        wvp = wv.reshape(16, 128, 8, 256).transpose(2, 1, 0, 3).reshape(1024, 4096)
        wg8p = wg8.reshape(16, 128, 32, 256).transpose(2, 1, 0, 3).reshape(4096, 4096)
        wu8p = wu8.reshape(16, 128, 32, 256).transpose(2, 1, 0, 3).reshape(4096, 4096)
        wdp = wd8.reshape(16, 4, 128, 2, 1024).transpose(3, 0, 2, 1, 4).reshape(4096, 4096)
        w1p = w1.reshape(22, 256, 2, 8, 128).transpose(0, 2, 4, 3, 1).reshape(5632, 2048)
        w2p = w2.reshape(22, 256, 2, 8, 128).transpose(0, 2, 4, 3, 1).reshape(5632, 2048)
        w3p = w3.reshape(4, 512, 44, 128).transpose(2, 3, 0, 1).reshape(5632, 2048)
        qn4 = wqn.reshape(12, 128, 8, 128).transpose(2, 0, 1, 3)
        qr4 = wqr.reshape(12, 128, 8, 64).transpose(2, 0, 1, 3)
        wqwp = np.concatenate([qn4, qr4], axis=3)
        wqwp = wqwp.reshape(8, 2, 6, 128, 192).transpose(0, 1, 3, 2, 4).reshape(2048, 1152)
        kn4 = wkn.reshape(4, 128, 8, 128).transpose(2, 1, 0, 3).reshape(8, 128, 512)
        kr4 = wkr.reshape(4, 128, 8, 64).transpose(2, 1, 0, 3).reshape(8, 128, 256)
        wkwp = np.concatenate([kn4, kr4], axis=2).reshape(1024, 768)
        wvwp = wvu.reshape(4, 128, 8, 192).transpose(2, 1, 0, 3).reshape(1024, 768)
        m = {
            "x_in": x[b, hf * 1024:(hf + 1) * 1024, :].T,
            "wqp": wqp, "wkp": wkp, "wvp": wvp, "wo0": inp["L0_attn_Wo"],
            "bo_t": np.asarray(inp["L0_attn_bo"]).reshape(16, 128).T,
            "wg8p": wg8p, "wu8p": wu8p, "wdp": wdp,
            "sg_t": np.asarray(inp["L0_ffn_s_gate"]).reshape(64, 128).T,
            "su_t": np.asarray(inp["L0_ffn_s_up"]).reshape(64, 128).T,
            "sd_t": np.asarray(inp["L0_ffn_s_down"]).reshape(16, 128).T,
            "wqd": inp["L1_attn_wq_down"], "wkvd": inp["L1_attn_wkv_down"],
            "wqwp": wqwp, "wkwp": wkwp, "wvwp": wvwp,
            "wo1": inp["L1_attn_wo"],
            "w1p": w1p, "w2p": w2p, "w3p": w3p,
            "tok_off": np.array([[hf * 1024]], np.int32),
            "ident_bf": ident.astype(BF), "ones_f": ones,
            "cosd": cosd, "sind": sind,
            "mask_e": mask_e.astype(BF), "mask_o": mask_o.astype(BF),
        }
        in_maps.append({k: np.ascontiguousarray(np.asarray(v)) for k, v in m.items()})
    return in_maps


def kernel(**inp):
    nc = build_program()
    in_maps = make_in_maps(inp)
    res = run_bass_kernel_spmd(nc, in_maps, core_ids=list(range(8)))
    out = np.empty((4, 2048, 2048), np.float32)
    for c in range(8):
        b, hf = c // 2, c % 2
        out[b, hf * 1024:(hf + 1) * 1024, :] = np.asarray(res.results[c]["y"]).T
    return out
